# version 13
# baseline (speedup 1.0000x reference)
import numpy as np
import ml_dtypes
from contextlib import ExitStack
import concourse.bass as bass
import concourse.mybir as mybir
from concourse.bass_utils import run_bass_kernel_spmd

F32 = mybir.dt.float32
BF16 = mybir.dt.bfloat16
I8 = mybir.dt.int8
AF = mybir.ActivationFunctionType
ALU = mybir.AluOpType
AXX = mybir.AxisListType.X

B, N, F, H, HD = 4, 4096, 1024, 16, 64
TOWN = N // 2
NC = 8
CK_KV = 512
CK_OWN = 256
NKV = N // CK_KV
NOWN = TOWN // CK_OWN
NPAIR = 8
NQC = 4
NJT = 32

ENGINES = ("sync", "gpsimd", "tensor", "scalar", "vector")


class Sched:
    def __init__(self):
        self.ev = {e: [] for e in ENGINES}
        self.ct = {}

    def cur(self, sem):
        return self.ct.get(sem, 0)

    def emit(self, eng, fn, waits=(), inc=None, amt=1):
        tok = None
        if inc is not None:
            tok = self.ct.get(inc, 0) + amt
            self.ct[inc] = tok
        self.ev[eng].append((fn, tuple((sn, v) for sn, v in waits if v and v > 0),
                             inc, amt))
        return tok


def build_nc(dbg=False):
    nc = bass.Bass("TRN2", target_bir_lowering=False, debug=False)

    XOWN = nc.dram_tensor("XOWN", [TOWN, F], F32, kind="ExternalInput").ap()
    XKV = nc.dram_tensor("XKV", [N, F], F32, kind="ExternalInput").ap()
    WQT = nc.dram_tensor("WQT", [128, 8192], F32, kind="ExternalInput").ap()
    WKT = nc.dram_tensor("WKT", [128, 8192], F32, kind="ExternalInput").ap()
    WVT = nc.dram_tensor("WVT", [128, 8192], F32, kind="ExternalInput").ap()
    WOT = nc.dram_tensor("WOT", [128, 8192], F32, kind="ExternalInput").ap()
    W8T = nc.dram_tensor("W8T", [128, 8192], I8, kind="ExternalInput").ap()
    IDF = nc.dram_tensor("IDF", [128, 128], F32, kind="ExternalInput").ap()
    IDB = nc.dram_tensor("IDB", [128, 128], BF16, kind="ExternalInput").ap()
    ONESF = nc.dram_tensor("ONESF", [1, 128], F32, kind="ExternalInput").ap()
    INDEV = nc.dram_tensor("INDEV", [128, 1], BF16, kind="ExternalInput").ap()
    INDOD = nc.dram_tensor("INDOD", [128, 1], BF16, kind="ExternalInput").ap()
    ONESB = nc.dram_tensor("ONESB", [128, 1], BF16, kind="ExternalInput").ap()
    BQK = nc.dram_tensor("BQK", [128, 16], F32, kind="ExternalInput").ap()
    BVSH = nc.dram_tensor("BVSH", [128, 16], F32, kind="ExternalInput").ap()
    BOREP = nc.dram_tensor("BOREP", [128, 1024], F32, kind="ExternalInput").ap()
    WSREP = nc.dram_tensor("WSREP", [128, 1024], F32, kind="ExternalInput").ap()
    BIREP = nc.dram_tensor("BIREP", [128, 1024], F32, kind="ExternalInput").ap()
    OUT = nc.dram_tensor("OUT", [TOWN, F], F32, kind="ExternalOutput").ap()

    kd = dict(kind="ExternalOutput") if dbg else {}
    sc_k = nc.dram_tensor("sc_k", [8, 128, 2, N], BF16, **kd).ap()
    sc_q = nc.dram_tensor("sc_q", [8, 128, 2, TOWN], BF16, **kd).ap()
    sc_v = nc.dram_tensor("sc_v", [N // 128, 128, 2, 1024], BF16, **kd).ap()
    sc_o = nc.dram_tensor("sc_o", [16, 64, 2, TOWN], BF16, **kd).ap()
    sc_xq = nc.dram_tensor("sc_xq", [8, 128, TOWN], BF16, **kd).ap()
    sc_den = nc.dram_tensor("sc_den", [8, 4, 1024], F32, **kd).ap()

    idf = nc.alloc_sbuf_tensor("idf", [128, 128], F32).ap()
    idb = nc.alloc_sbuf_tensor("idb", [128, 128], BF16).ap()
    onesf = nc.alloc_sbuf_tensor("onesf", [1, 128], F32).ap()
    indev = nc.alloc_sbuf_tensor("indev", [128, 1], BF16).ap()
    indod = nc.alloc_sbuf_tensor("indod", [128, 1], BF16).ap()
    onesb = nc.alloc_sbuf_tensor("onesb", [128, 1], BF16).ap()
    bqk = nc.alloc_sbuf_tensor("bqk", [128, 16], F32).ap()
    bvsh = nc.alloc_sbuf_tensor("bvsh", [128, 16], F32).ap()
    mneg = nc.alloc_sbuf_tensor("mneg", [128, 16], F32).ap()
    xsr = nc.alloc_sbuf_tensor("xsr", [128, 16], F32).ap()
    stats = nc.alloc_sbuf_tensor("stats", [128, 64], F32).ap()
    nmax = nc.alloc_sbuf_tensor("nmax", [128, 448], F32).ap()
    wbf = nc.alloc_sbuf_tensor("wbf", [128, 69632], BF16).ap()
    fpool = nc.alloc_sbuf_tensor("fpool", [128, 15360], F32).ap()
    banks = [nc.alloc_psum_tensor("bank%d" % i, [128, 512], F32).ap()
             for i in range(8)]

    def bsl(off, n):
        return wbf[:, off:off + n]

    def fsl(off, n):
        return fpool[:, off:off + n]

    s = Sched()

    tr_toks = []
    kq_toks = []
    vb_toks = []
    nk_toks = []
    nk_toks_t = []
    kst_toks = []
    qst_toks = []
    vst_toks = []
    xqst_toks = []

    def rot(lst, d=2):
        return lst[-d] if len(lst) >= d else None

    for dst, src in ((idf, IDF), (idb, IDB), (onesf, ONESF), (indev, INDEV),
                     (indod, INDOD), (onesb, ONESB), (bqk, BQK), (bvsh, BVSH)):
        s.emit("sync", (lambda d_, sr: lambda e: e.dma_start(out=d_, in_=sr))(dst, src),
               inc="LDC", amt=16)

    wstage = fsl(0, 8192)
    for wi, (WSRC, hi_off, lo_off) in enumerate(
            ((WQT, 0, 8192), (WKT, 16384, 24576), (WVT, 32768, 40960))):
        waits = [("VW", s.cur("VW"))] if wi > 0 else []
        s.emit("sync", (lambda sr: lambda e: e.dma_start(out=wstage, in_=sr))(WSRC),
               waits=waits, inc="LDW", amt=16)
        ldtok = s.cur("LDW")
        hi, lo = bsl(hi_off, 8192), bsl(lo_off, 8192)
        s.emit("vector", (lambda h: lambda e: e.tensor_copy(h, wstage))(hi),
               waits=[("LDW", ldtok)])
        s.emit("vector", lambda e: e.drain())
        s.emit("vector", (lambda l, h: lambda e: e.tensor_sub(l, wstage, h))(lo, hi),
               inc="VW")
    vw_done = s.cur("VW")

    def emit_ln(xb, xnb, ntiles, ldw_pair, sq_out, v_sem, extra_sc_waits=()):
        nt = ntiles
        ldsem, ld_tok = ldw_pair
        for i in range(nt):
            s.emit("vector",
                   (lambda i: lambda e: e.tensor_reduce(
                       stats[:, 0 + i:1 + i], xb[:, i * 1024:(i + 1) * 1024],
                       axis=AXX, op=ALU.add))(i),
                   waits=[(ldsem, ld_tok)] if i == 0 else ())
        for i in range(nt):
            if i > 0:
                s.emit("scalar", lambda e: e.drain())
            s.emit("scalar",
                   (lambda i: lambda e: e.activation(
                       sq_out[:, 0:1024], xb[:, i * 1024:(i + 1) * 1024],
                       AF.Square, accum_out=stats[:, 8 + i:9 + i]))(i),
                   waits=(list(extra_sc_waits) + [(ldsem, ld_tok)]) if i == 0 else (),
                   inc=("SCLN" if i == nt - 1 else None))
        sct = s.cur("SCLN")
        s.emit("vector", lambda e: e.drain())
        s.emit("vector", (lambda nt: lambda e: e.tensor_scalar(
            stats[:, 16:16 + nt], stats[:, 0:nt], 1.0 / F, None, ALU.mult))(nt),
            waits=[("SCLN", sct)])
        s.emit("vector", (lambda nt: lambda e: e.tensor_scalar(
            stats[:, 32:32 + nt], stats[:, 8:8 + nt], 1.0 / F, None, ALU.mult))(nt))
        s.emit("vector", lambda e: e.drain())
        s.emit("vector", (lambda nt: lambda e: e.tensor_mul(
            stats[:, 24:24 + nt], stats[:, 16:16 + nt], stats[:, 16:16 + nt]))(nt))
        s.emit("vector", lambda e: e.drain())
        s.emit("vector", (lambda nt: lambda e: e.tensor_sub(
            stats[:, 32:32 + nt], stats[:, 32:32 + nt], stats[:, 24:24 + nt]))(nt))
        s.emit("vector", lambda e: e.drain())
        s.emit("vector", (lambda nt: lambda e: e.tensor_scalar(
            stats[:, 40:40 + nt], stats[:, 32:32 + nt], 1e-5, None, ALU.add))(nt))
        s.emit("vector", lambda e: e.drain())
        s.emit("vector", (lambda nt: lambda e: e.reciprocal(
            stats[:, 48:48 + nt], stats[:, 40:40 + nt]))(nt), inc="VRC")
        vrc = s.cur("VRC")
        s.emit("scalar", (lambda nt: lambda e: e.activation(
            stats[:, 56:56 + nt], stats[:, 48:48 + nt], AF.Sqrt))(nt),
            waits=[("VRC", vrc)], inc="SCLN")
        sct = s.cur("SCLN")
        for it in range(1):
            s.emit("vector", (lambda nt: lambda e: e.tensor_mul(
                stats[:, 48:48 + nt], stats[:, 56:56 + nt], stats[:, 56:56 + nt]))(nt),
                waits=[("SCLN", sct)] if it == 0 else ())
            s.emit("vector", lambda e: e.drain())
            s.emit("vector", (lambda nt: lambda e: e.tensor_mul(
                stats[:, 48:48 + nt], stats[:, 48:48 + nt], stats[:, 40:40 + nt]))(nt))
            s.emit("vector", lambda e: e.drain())
            s.emit("vector", (lambda nt: lambda e: e.tensor_scalar(
                stats[:, 48:48 + nt], stats[:, 48:48 + nt], -0.5, 1.5,
                ALU.mult, ALU.add))(nt))
            s.emit("vector", lambda e: e.drain())
            s.emit("vector", (lambda nt: lambda e: e.tensor_mul(
                stats[:, 56:56 + nt], stats[:, 56:56 + nt], stats[:, 48:48 + nt]))(nt))
            s.emit("vector", lambda e: e.drain())
        for i in range(nt):
            s.emit("vector",
                   (lambda i: lambda e: e.tensor_scalar(
                       xnb[:, i * 1024:(i + 1) * 1024],
                       xb[:, i * 1024:(i + 1) * 1024],
                       stats[:, 16 + i:17 + i], stats[:, 56 + i:57 + i],
                       ALU.subtract, ALU.mult))(i),
                   waits=[("TT", s.cur("TT")), ("SCLN", sct)] if i == 0 else (),
                   inc=(v_sem if i == nt - 1 else None))
        return s.cur(v_sem)

    def emit_transposes(xnb, ntiles, first_waits, hi_dst, lo_dst, w):
        for fc in range(8):
            bk = banks[len(tr_toks) % 2]
            for i in range(ntiles):
                wl = []
                if i == 0:
                    if fc == 0:
                        wl += list(first_waits)
                    t = rot(tr_toks)
                    if t:
                        wl.append(("VT", t))
                s.emit("tensor",
                       (lambda fc, i, bk: lambda e: e.transpose(
                           bk[:, i * 128:(i + 1) * 128],
                           xnb[:, i * 1024 + fc * 128: i * 1024 + fc * 128 + 128],
                           idf))(fc, i, bk),
                       waits=wl, inc=("TT" if i == ntiles - 1 else None))
            tt = s.cur("TT")
            scw = [("TT", tt)]
            if fc == 0:
                scw += [("TK", s.cur("TK")), ("TV", s.cur("TV"))]
            s.emit("scalar",
                   (lambda fc, bk, w: lambda e: e.activation(
                       hi_dst[:, fc * w:fc * w + w], bk[:, 0:w], AF.Copy))(bk=bk, fc=fc, w=w),
                   waits=scw, inc="SCT")
            sct = s.cur("SCT")
            s.emit("vector",
                   (lambda fc, bk, w: lambda e: e.tensor_sub(
                       lo_dst[:, fc * w:fc * w + w], bk[:, 0:w],
                       hi_dst[:, fc * w:fc * w + w]))(bk=bk, fc=fc, w=w),
                   waits=[("SCT", sct)], inc="VT")
            tr_toks.append(s.cur("VT"))

    xbuf = fsl(0, 2048)
    xnbuf = fsl(2048, 2048)
    sqbuf = fsl(4096, 1024)
    xbuf2 = fsl(5120, 2048)
    xnbuf2 = fsl(7168, 2048)
    t0buf = fsl(9216, 2048)
    ubuf = fsl(11264, 2048)
    bbuf = fsl(13312, 2048)
    xnT_hi = bsl(49152, 2048)
    xnT_lo = bsl(51200, 2048)
    xoT_hi = bsl(53248, 2048)
    xoT_lo = bsl(55296, 2048)
    kstage = [bsl(57344, 1024), bsl(58368, 1024)]
    vstage = [bsl(59392, 2048), bsl(61440, 2048)]
    ksq = [bsl(63488, 512), bsl(64000, 512)]
    xqbf = bsl(64512, 2048)
    qstage = [bsl(66560, 1024), bsl(67584, 1024)]
    xqst = [bsl(68608, 512), bsl(69120, 512)]
    nsq_i = [0]
    KHI, KLO, VHI, VLO = 16384, 24576, 32768, 40960

    vtasks = []

    def flush_vtask(n=1):
        for _ in range(n):
            if vtasks:
                vtasks.pop(0)()

    def emit_nsq(sqsrc, width, cols, pend_nk):
        def go():
            for par, ind in ((0, indev), (1, indod)):
                bnk = banks[6 + len(nk_toks) % 2]
                wl = []
                t = rot(nk_toks)
                if t:
                    wl.append(("VNK", t))
                s.emit("tensor",
                       (lambda ind, bnk, width: lambda e: e.matmul(
                           bnk[0:1, 0:width], ind, sqsrc[:, 0:width],
                           start=True, stop=True))(ind, bnk, width),
                       waits=wl + [("SCQ", pend_nk)], inc="TNK")
                tnk = s.cur("TNK")
                col = cols[par]
                s.emit("vector",
                       (lambda bnk, col, width: lambda e: e.tensor_reduce(
                           nmax[0:1, col:col + 1], bnk[0:1, 0:width], axis=AXX,
                           op=ALU.max))(bnk, col, width),
                       waits=[("TNK", tnk)], inc="VNK")
                nk_toks.append(s.cur("VNK"))
            nk_toks_t.append(s.cur("TNK"))
        return go

    NKV2 = 16
    pending_nk = None
    vqz_toks = []
    xqtr_toks = []
    for c in range(NKV2):
        waits = ([("VW", vw_done), ("LDC", s.cur("LDC"))] if c == 0
                 else [("VKV", s.cur("VKV")), ("SCLN", s.cur("SCLN")),
                       ("VLAD", s.cur("VLAD"))])
        s.emit("sync",
               (lambda c: lambda e: e.dma_start(
                   out=xbuf.rearrange("p (i f) -> p i f", i=2),
                   in_=XKV.rearrange("(c i p) f -> c i p f", i=2, p=128)[c]
                   .transpose([1, 0, 2])))(c),
               waits=waits, inc="LDXK", amt=16)
        ld_tok = s.cur("LDXK")
        vkv = emit_ln(xbuf, xnbuf, 2, ("LDXK", ld_tok), sqbuf, "VKV")
        emit_transposes(xnbuf, 2, [("VKV", vkv)], xnT_hi, xnT_lo, 256)
        vxt = s.cur("VT")

        for dc in range(8):
            bk = banks[2 + len(kq_toks) % 2]
            n_mm = 0
            for fc in range(8):
                for (woff, xsrc) in ((KHI, xnT_hi), (KLO, xnT_hi), (KHI, xnT_lo)):
                    st, sp = n_mm == 0, n_mm == 23
                    wl = []
                    if st:
                        wl = [("VT", vxt)]
                        t = rot(kq_toks)
                        if t:
                            wl.append(("VK", t))
                    s.emit("tensor",
                           (lambda bk, woff, fc, dc, xsrc, st, sp: lambda e:
                            e.matmul(bk[:, 0:256],
                                     wbf[:, woff + fc * 1024 + dc * 128:
                                         woff + fc * 1024 + dc * 128 + 128],
                                     xsrc[:, fc * 256:(fc + 1) * 256],
                                     start=st, stop=sp))(bk, woff, fc, dc, xsrc, st, sp),
                           waits=wl, inc=("TK" if sp else None))
                    n_mm += 1
            if pending_nk is not None:
                pending_nk()
                pending_nk = None
            tk = s.cur("TK")
            kst = kstage[dc % 2]
            wl = [("TK", tk)]
            t = rot(kst_toks)
            if t:
                wl.append(("SPK%d" % (dc % 2), t))
            s.emit("scalar",
                   (lambda kst, bk, dc: lambda e: e.activation(
                       kst[:, 0:256], bk[:, 0:256], AF.Identity,
                       bias=bqk[:, 8 + dc:9 + dc], scale=1.0))(kst, bk, dc),
                   waits=wl, inc="SCK")
            sck = s.cur("SCK")
            sqb_ = ksq[nsq_i[0] % 2]
            nsq_i[0] += 1
            s.emit("scalar", lambda e: e.drain())
            s.emit("scalar",
                   (lambda kst, sqb_: lambda e: e.activation(
                       sqb_[:, 0:256], kst[:, 0:256], AF.Square))(kst, sqb_),
                   waits=[("TNK", rot(nk_toks_t, 2) or 0)], inc="SCQ")
            scq = s.cur("SCQ")
            s.emit("vector",
                   (lambda kst, bk, dc: lambda e: e.scalar_tensor_tensor(
                       kst[:, 512:768], bk[:, 0:256], bqk[:, 8 + dc:9 + dc],
                       kst[:, 0:256], ALU.add, ALU.subtract))(kst, bk, dc),
                   waits=[("SCK", sck)], inc="VK")
            kq_toks.append(s.cur("VK"))
            vk = s.cur("VK")
            flush_vtask()
            pending_nk = emit_nsq(sqb_, 256,
                                  ((2 * dc) * 16 + c, (2 * dc + 1) * 16 + c), scq)
            s.emit("gpsimd",
                   (lambda kst, dc, c: lambda e: e.dma_start(
                       out=sc_k[dc, :, :, c * 256:(c + 1) * 256],
                       in_=kst.rearrange("p (l t) -> p l t", l=2)[:, :, 0:256]))(kst, dc, c),
                   waits=[("VK", vk)], inc="SPK%d" % (dc % 2), amt=16)
            kst_toks.append(s.cur("SPK%d" % (dc % 2)))
        if pending_nk is not None:
            pending_nk()
            pending_nk = None

        for i in range(2):
            gi = c * 2 + i
            vst = vstage[gi % 2]
            for half in range(2):
                bk = banks[4 + len(vb_toks) % 2]
                n_mm = 0
                for fc in range(8):
                    for (xsrc, woff) in ((xnT_hi, VHI), (xnT_lo, VHI), (xnT_hi, VLO)):
                        st, sp = n_mm == 0, n_mm == 23
                        wl = []
                        if st:
                            t = rot(vb_toks)
                            if t:
                                wl.append(("VV", t))
                        s.emit("tensor",
                               (lambda bk, xsrc, fc, i, woff, half, st, sp: lambda e:
                                e.matmul(bk, xsrc[:, fc * 256 + i * 128:
                                                  fc * 256 + i * 128 + 128],
                                         wbf[:, woff + fc * 1024 + half * 512:
                                             woff + fc * 1024 + half * 512 + 512],
                                         start=st, stop=sp))(bk, xsrc, fc, i, woff,
                                                             half, st, sp),
                               waits=wl, inc=("TV" if sp else None))
                        n_mm += 1
                tv = s.cur("TV")
                wl = [("TV", tv)]
                t = rot(vst_toks)
                if t and half == 0:
                    wl.append(("SPV%d" % (gi % 2), t))
                s.emit("scalar",
                       (lambda vst, bk, half: lambda e: e.activation(
                           vst[:, half * 512:half * 512 + 512], bk,
                           AF.Copy))(vst, bk, half),
                       waits=wl, inc="SCV")
                scv = s.cur("SCV")
                s.emit("vector",
                       (lambda vst, bk, half: lambda e: e.tensor_sub(
                           vst[:, 1024 + half * 512:1024 + half * 512 + 512],
                           bk, vst[:, half * 512:half * 512 + 512]))(vst, bk, half),
                       waits=[("SCV", scv)], inc="VV")
                vb_toks.append(s.cur("VV"))
                flush_vtask()
            vv = s.cur("VV")
            s.emit("gpsimd",
                   (lambda vst, gi: lambda e: e.dma_start(
                       out=sc_v[gi],
                       in_=vst.rearrange("p (l d) -> p l d", l=2)))(vst, gi),
                   waits=[("VV", vv)], inc="SPV%d" % (gi % 2), amt=16)
            vst_toks.append(s.cur("SPV%d" % (gi % 2)))

        if c % 2 == 0:
            continue
        oc = c // 2
        ow = [("VKV", s.cur("VKV")), ("LDC", s.cur("LDC"))]
        if oc > 0:
            ow += [("VLAD", s.cur("VLAD")), ("SCLN", s.cur("SCLN"))]
        s.emit("sync",
               (lambda oc: lambda e: e.dma_start(
                   out=xbuf2.rearrange("p (i f) -> p i f", i=2),
                   in_=XOWN.rearrange("(c i p) f -> c i p f", i=2, p=128)[oc]
                   .transpose([1, 0, 2])))(oc),
               waits=ow, inc="LDXO", amt=16)
        ld_tok = s.cur("LDXO")
        vown = emit_ln(xbuf2, xnbuf2, 2, ("LDXO", ld_tok), sqbuf, "VOWN")
        s.emit("vector", lambda e: e.drain())
        for i in range(2):
            s.emit("vector",
                   (lambda i: lambda e: e.tensor_reduce(
                       stats[:, 0 + i:1 + i], xnbuf2[:, i * 1024:(i + 1) * 1024],
                       axis=AXX, op=ALU.max, apply_absolute_value=True))(i))
        s.emit("vector", lambda e: e.drain())
        s.emit("vector",
               (lambda oc: lambda e: e.tensor_scalar(
                   xsr[:, oc * 2:oc * 2 + 2], stats[:, 0:2], 1e-6, 1.0 / 127.0,
                   ALU.max, ALU.mult))(oc))
        s.emit("vector", lambda e: e.drain())
        s.emit("vector",
               (lambda oc: lambda e: e.reciprocal(
                   stats[:, 12:14], xsr[:, oc * 2:oc * 2 + 2]))(oc))
        s.emit("vector", lambda e: e.drain())
        for i in range(2):
            s.emit("vector",
                   (lambda i: lambda e: e.tensor_scalar(
                       t0buf[:, i * 1024:(i + 1) * 1024],
                       xnbuf2[:, i * 1024:(i + 1) * 1024],
                       stats[:, 12 + i:13 + i], 128.5, ALU.mult, ALU.add))(i),
                   inc=("VLAD" if i == 1 else None))
        emit_transposes(xnbuf2, 2, [("VOWN", vown)], xoT_hi, xoT_lo, 256)
        vxt = s.cur("VT")

        def mk_piece(fn):
            vtasks.append(fn)

        mk_piece(lambda: (s.emit("vector", lambda e: e.drain()),
                          s.emit("vector", lambda e: e.tensor_copy(ubuf, t0buf)),
                          s.emit("vector", lambda e: e.drain())))
        for cst in (128.0, 64.0, 32.0, 16.0, 8.0, 4.0, 2.0, 1.0):
            mk_piece((lambda cst: lambda: (
                s.emit("vector",
                       (lambda cst: lambda e: e.tensor_scalar(
                           bbuf, ubuf, cst, None, ALU.is_ge))(cst)),
                s.emit("vector", lambda e: e.drain())))(cst))
            mk_piece((lambda cst: lambda: (
                s.emit("vector",
                       (lambda cst: lambda e: e.scalar_tensor_tensor(
                           ubuf, bbuf, -cst, ubuf, ALU.mult, ALU.add))(cst)),
                s.emit("vector", lambda e: e.drain())))(cst))

        def xq_piece(oc=oc):
            wl = [("TT", xqtr_toks[oc - 1])] if oc > 0 else []
            s.emit("vector", (lambda: lambda e: e.scalar_tensor_tensor(
                xqbf, t0buf, -128.0, ubuf, ALU.add, ALU.subtract))(),
                waits=wl, inc="VQZ")
            vqz_toks.append(s.cur("VQZ"))
        mk_piece(xq_piece)

        for dc in range(8):
            bk = banks[2 + len(kq_toks) % 2]
            n_mm = 0
            for fc in range(8):
                for (woff, xsrc) in ((0, xoT_hi), (8192, xoT_hi), (0, xoT_lo)):
                    st, sp = n_mm == 0, n_mm == 23
                    wl = []
                    if st:
                        wl = [("VT", vxt)]
                        t = rot(kq_toks)
                        if t:
                            wl.append(("VK", t))
                    s.emit("tensor",
                           (lambda bk, woff, fc, dc, xsrc, st, sp: lambda e:
                            e.matmul(bk[:, 0:256],
                                     wbf[:, woff + fc * 1024 + dc * 128:
                                         woff + fc * 1024 + dc * 128 + 128],
                                     xsrc[:, fc * 256:(fc + 1) * 256],
                                     start=st, stop=sp))(bk, woff, fc, dc, xsrc, st, sp),
                           waits=wl, inc=("TK" if sp else None))
                    n_mm += 1
            if pending_nk is not None:
                pending_nk()
                pending_nk = None
            tk = s.cur("TK")
            qst = qstage[dc % 2]
            wl = [("TK", tk)]
            t = rot(qst_toks)
            if t:
                wl.append(("SPQ%d" % (dc % 2), t))
            s.emit("scalar",
                   (lambda qst, bk, dc: lambda e: e.activation(
                       qst[:, 0:256], bk[:, 0:256], AF.Identity,
                       bias=bqk[:, dc:dc + 1], scale=1.0))(qst, bk, dc),
                   waits=wl, inc="SCK")
            sck = s.cur("SCK")
            sqb_ = ksq[nsq_i[0] % 2]
            nsq_i[0] += 1
            s.emit("scalar", lambda e: e.drain())
            s.emit("scalar",
                   (lambda qst, sqb_: lambda e: e.activation(
                       sqb_[:, 0:256], qst[:, 0:256], AF.Square))(qst, sqb_),
                   waits=[("TNK", rot(nk_toks_t, 2) or 0)], inc="SCQ")
            scq = s.cur("SCQ")
            s.emit("vector",
                   (lambda qst, bk, dc: lambda e: e.scalar_tensor_tensor(
                       qst[:, 512:768], bk[:, 0:256], bqk[:, dc:dc + 1],
                       qst[:, 0:256], ALU.add, ALU.subtract))(qst, bk, dc),
                   waits=[("SCK", sck)], inc="VK")
            kq_toks.append(s.cur("VK"))
            vk = s.cur("VK")
            flush_vtask()
            pending_nk = emit_nsq(sqb_, 256,
                                  (256 + (2 * dc) * 8 + oc,
                                   256 + (2 * dc + 1) * 8 + oc), scq)
            s.emit("gpsimd",
                   (lambda qst, dc, oc: lambda e: e.dma_start(
                       out=sc_q[dc, :, :, oc * 256:(oc + 1) * 256],
                       in_=qst.rearrange("p (l t) -> p l t", l=2)[:, :, 0:256]))(qst, dc, oc),
                   waits=[("VK", vk)], inc="SPQ%d" % (dc % 2), amt=16)
            qst_toks.append(s.cur("SPQ%d" % (dc % 2)))
        if pending_nk is not None:
            pending_nk()
            pending_nk = None

        if oc > 0:
            cc = oc - 1
            for fc in range(8):
                bk = banks[len(tr_toks) % 2].bitcast(BF16)
                for i in range(2):
                    wl = []
                    if i == 0:
                        if fc == 0:
                            wl.append(("VQZ", vqz_toks[cc]))
                        t = rot(tr_toks)
                        if t:
                            wl.append(("VT", t))
                    s.emit("tensor",
                           (lambda fc, i, bk, cc: lambda e: e.transpose(
                               bk[:, i * 128:(i + 1) * 128],
                               xqbf[:, i * 1024 + fc * 128: i * 1024 + fc * 128 + 128],
                               idb))(fc, i, bk, cc),
                           waits=wl, inc=("TT" if i == 1 else None))
                tt = s.cur("TT")
                xst = xqst[fc % 2]
                wl = [("TT", tt)]
                t = rot(xqst_toks)
                if t:
                    wl.append(("SPXQ%d" % (fc % 2), t))
                s.emit("vector",
                       (lambda bk, xst: lambda e: e.tensor_copy(
                           xst[:, 0:256], bk[:, 0:256]))(bk, xst),
                       waits=wl, inc="VT")
                tr_toks.append(s.cur("VT"))
                vt = s.cur("VT")
                s.emit("gpsimd",
                       (lambda fc, cc, xst: lambda e: e.dma_start(
                           out=sc_xq[fc, :, cc * 256:(cc + 1) * 256],
                           in_=xst[:, 0:256]))(fc, cc, xst),
                       waits=[("VT", vt)], inc="SPXQ%d" % (fc % 2), amt=16)
                xqst_toks.append(s.cur("SPXQ%d" % (fc % 2)))
            xqtr_toks.append(s.cur("TT"))

    while vtasks:
        flush_vtask()
    cc = 7
    for fc in range(8):
        bk = banks[len(tr_toks) % 2].bitcast(BF16)
        for i in range(2):
            wl = []
            if i == 0:
                if fc == 0:
                    wl.append(("VQZ", vqz_toks[cc]))
                t = rot(tr_toks)
                if t:
                    wl.append(("VT", t))
            s.emit("tensor",
                   (lambda fc, i, bk: lambda e: e.transpose(
                       bk[:, i * 128:(i + 1) * 128],
                       xqbf[:, i * 1024 + fc * 128: i * 1024 + fc * 128 + 128],
                       idb))(fc, i, bk),
                   waits=wl, inc=("TT" if i == 1 else None))
        tt = s.cur("TT")
        xst = xqst[fc % 2]
        wl = [("TT", tt)]
        t = rot(xqst_toks)
        if t:
            wl.append(("SPXQ%d" % (fc % 2), t))
        s.emit("vector",
               (lambda bk, xst: lambda e: e.tensor_copy(
                   xst[:, 0:256], bk[:, 0:256]))(bk, xst),
               waits=wl, inc="VT")
        tr_toks.append(s.cur("VT"))
        vt = s.cur("VT")
        s.emit("gpsimd",
               (lambda fc, xst: lambda e: e.dma_start(
                   out=sc_xq[fc, :, cc * 256:(cc + 1) * 256],
                   in_=xst[:, 0:256]))(fc, xst),
               waits=[("VT", vt)], inc="SPXQ%d" % (fc % 2), amt=16)
        xqst_toks.append(s.cur("SPXQ%d" % (fc % 2)))

    vnk_all = s.cur("VNK")
    s.emit("vector", lambda e: e.tensor_reduce(
        nmax[0:1, 384:400].rearrange("p (h o) -> p h o", o=1),
        nmax[0:1, 0:256].rearrange("p (h c) -> p h c", h=16), axis=AXX,
        op=ALU.max), waits=[("VNK", vnk_all)])
    s.emit("vector", lambda e: e.tensor_reduce(
        nmax[0:1, 400:416].rearrange("p (h o) -> p h o", o=1),
        nmax[0:1, 256:384].rearrange("p (h c) -> p h c", h=16), axis=AXX,
        op=ALU.max))
    s.emit("vector", lambda e: e.drain())
    s.emit("vector", lambda e: e.tensor_add(
        nmax[0:1, 416:432], nmax[0:1, 384:400], nmax[0:1, 400:416]))
    s.emit("vector", lambda e: e.drain())
    s.emit("vector", lambda e: e.tensor_scalar(
        nmax[0:1, 432:448], nmax[0:1, 416:432], -1.02 / 64.0, -0.05,
        ALU.mult, ALU.add), inc="VM")
    vm = s.cur("VM")
    s.emit("tensor", lambda e: e.matmul(
        banks[6][:, 0:16], onesf[0:1, :], nmax[0:1, 432:448],
        start=True, stop=True),
        waits=[("VM", vm), ("VNK", s.cur("VNK"))], inc="TM")
    tm = s.cur("TM")
    s.emit("vector", lambda e: e.tensor_copy(mneg, banks[6][:, 0:16]),
           waits=[("TM", tm)], inc="VM")
    vm_done = s.cur("VM")

    kb = [bsl(0, 8192), bsl(8192, 8192)]
    qb = [bsl(16384, 4096), bsl(20480, 4096)]
    vbh = [bsl(24576, 4160), bsl(28736, 4160)]
    vbl = [bsl(32896, 4160), bsl(37056, 4160)]
    Ph = [[bsl(41216 + hh * 2048 + r * 512, 512) for r in range(4)]
          for hh in range(2)]
    Pl = [[bsl(57344 + hh * 2048 + r * 512, 512) for r in range(4)]
          for hh in range(2)]
    ost = [bsl(45312, 2048), bsl(47360, 2048)]
    Pf = [[fsl(hh * 2048 + r * 512, 512) for r in range(4)]
          for hh in range(2)]
    t1s = fsl(4096, 1024)
    rbs = fsl(5120, 1024)
    rden = fsl(6144, 1024)
    oun = [fsl(7168, 1024), fsl(8192, 1024)]
    dcp = fsl(9216, 1024)

    for bi_ in range(2):
        s.emit("vector", (lambda bi_: lambda e: e.memset(
            vbh[bi_].rearrange("p (h j c) -> p h j c", h=2, j=32)[:, :, :, 64:65],
            1.0))(bi_),
            waits=([("TK", s.cur("TK")), ("TV", s.cur("TV")),
                    ("TT", s.cur("TT"))] if bi_ == 0 else ()))
        s.emit("vector", (lambda bi_: lambda e: e.memset(
            vbl[bi_].rearrange("p (h j c) -> p h j c", h=2, j=32)[:, :, :, 64:65],
            0.0))(bi_), inc=("VONE" if bi_ == 1 else None))
    vones = s.cur("VONE")
    s.emit("vector", lambda e: e.nop(), waits=[("VM", vm_done)], inc="VMS")
    vms = s.cur("VMS")

    ta_toks = {}
    sce_g = []
    vp_g = []
    vph_g = []
    ta_g = []
    vo_toks = []
    voc_toks = []
    spd_toks = []
    spo_toks = []
    pend_evac = []

    def flush_evac():
        while pend_evac:
            pend_evac.pop(0)()

    for p in range(NPAIR):
        bi = p % 2
        w = [("VMS", vms), ("VM", vm_done),
             ("SPK0", s.cur("SPK0")), ("SPK1", s.cur("SPK1")),
             ("SPQ0", s.cur("SPQ0")), ("SPQ1", s.cur("SPQ1")),
             ("SPV0", s.cur("SPV0")), ("SPV1", s.cur("SPV1")),
             ("VNK", s.cur("VNK")), ("VT", s.cur("VT"))]
        if p >= 2:
            w.append(("TA", ta_toks[p - 2]))
        ldksem = "LDK%d" % bi
        s.emit("sync", (lambda p, bi: lambda e: e.dma_start(
            out=kb[bi], in_=sc_k[p].rearrange("p l t -> p (l t)")))(p, bi),
            waits=w, inc=ldksem, amt=16)
        s.emit("sync", (lambda p, bi: lambda e: e.dma_start(
            out=qb[bi], in_=sc_q[p].rearrange("p l t -> p (l t)")))(p, bi),
            inc=ldksem, amt=16)
        for hh in range(2):
            for lh, dstl in ((0, vbh), (1, vbl)):
                s.emit("sync", (lambda p, bi, hh, lh, dstl: lambda e: e.dma_start(
                    out=dstl[bi].rearrange("p (h j c) -> p h j c", h=2, j=32)
                    [:, hh, :, 0:64],
                    in_=sc_v[:, :, lh, p * 128 + hh * 64: p * 128 + hh * 64 + 64]
                    .transpose([1, 0, 2])))(p, bi, hh, lh, dstl),
                    inc=ldksem, amt=16)
        ldk = s.cur(ldksem)

        khi, klo = kb[bi][:, 0:4096], kb[bi][:, 4096:8192]
        qhi, qlo = qb[bi][:, 0:2048], qb[bi][:, 2048:4096]
        pend_av = []
        pend_rel = []
        for qc in range(NQC):

            def emit_av(avG, avjt, avqc, p=p, bi=bi):
                aw = [("VPH", vph_g[avG][0]), ("VPH2", vph_g[avG][1])]
                aw3 = [("VP", vp_g[avG][0]), ("VPB", vp_g[avG][1])]
                if avjt == 0:
                    idx = p * NQC + avqc
                    if idx >= 1:
                        aw.append(("VOC", voc_toks[idx - 1]))
                    if idx == 0:
                        aw += [("VV", s.cur("VV")), ("VNK", s.cur("VNK")),
                               ("TM", tm)]
                    if idx in (0, NQC):
                        aw.append(("VONE", vones))
                vh = vbh[bi].rearrange("p (h j c) -> p h j c", h=2, j=32)
                vl = vbl[bi].rearrange("p (h j c) -> p h j c", h=2, j=32)
                last = (avjt == NJT - 1)
                for ip, vv_ in enumerate((vh, vl, vh)):
                    for hh in range(2):
                        ob = banks[4 + hh]
                        pq = (Ph if ip < 2 else Pl)[hh][avG % 4]
                        wl2 = ()
                        if ip == 0 and hh == 0:
                            wl2 = aw
                        elif ip == 2 and hh == 0:
                            wl2 = aw3
                        s.emit("tensor",
                               (lambda ob, vv_, pq, hh, avjt, st, sp: lambda e:
                                e.matmul(ob[0:65, :],
                                         vv_[:, hh, avjt, 0:65], pq,
                                         start=st, stop=sp,
                                         tile_position=(0, 0)))(
                                   ob, vv_, pq, hh, avjt,
                                   (avjt == 0 and ip == 0),
                                   (last and ip == 2)),
                               waits=wl2,
                               inc=("TA" if (ip == 2 and hh == 1) else None))
                ta_g.append(s.cur("TA"))
            for jt in range(NJT):
                G = len(sce_g)
                sA = banks[jt % 2]
                sB = banks[2 + jt % 2]
                wl = [(ldksem, ldk)]
                if G >= 2:
                    wl.append(("VPH", vph_g[G - 2][0]))
                for ip, (kk, qq) in enumerate(((khi, qhi), (klo, qhi),
                                               (khi, qlo))):
                    for hh, (sbank, base, tp) in enumerate(
                            ((sA, 0, (0, 0)), (sB, 64, (64, 0)))):
                        st, sp = ip == 0, ip == 2
                        s.emit("tensor",
                               (lambda sbank, kk, qq, base, tp, jt, qc, st, sp:
                                lambda e: e.matmul(
                                    sbank,
                                    kk[base:base + 64, jt * 128:(jt + 1) * 128],
                                    qq[base:base + 64, qc * 512:(qc + 1) * 512],
                                    start=st, stop=sp, tile_position=tp))(
                                   sbank, kk, qq, base, tp, jt, qc, st, sp),
                               waits=(wl if (hh == 0 and ip == 0) else ()),
                               inc=("TS" if (hh == 1 and sp) else None))
                ts_tok = s.cur("TS")
                if len(pend_av) >= 2:
                    g0 = pend_av.pop(0)
                    emit_av(*g0)
                    if g0[1] == NJT - 1:
                        ta_toks[p] = s.cur("TA")
                        while pend_rel:
                            pend_rel.pop(0)()
                scw = [("TS", ts_tok)]
                if G >= 4:
                    scw.append(("VP", vp_g[G - 4][0]))
                    scw.append(("VPB", vp_g[G - 4][1]))
                    scw.append(("TA", ta_g[G - 4]))
                s.emit("scalar",
                       (lambda sbank, G, p: lambda e: e.activation(
                           Pf[0][G % 4], sbank, AF.Exp,
                           bias=mneg[:, 2 * p:2 * p + 1],
                           scale=0.03125))(sA, G, p),
                       waits=scw)
                s.emit("scalar",
                       (lambda sbank, G, p: lambda e: e.activation(
                           Pf[1][G % 4], sbank, AF.Exp,
                           bias=mneg[:, 2 * p + 1:2 * p + 2],
                           scale=0.03125))(sB, G, p),
                       inc="SCE")
                sce_g.append(s.cur("SCE"))
                s.emit("scalar",
                       (lambda sbank, G, p: lambda e: e.activation(
                           Ph[0][G % 4], sbank, AF.Exp,
                           bias=mneg[:, 2 * p:2 * p + 1],
                           scale=0.03125))(sA, G, p),
                       inc="VPH")
                vph = s.cur("VPH")
                dw = [("SCE", sce_g[-1])]
                if G >= 4:
                    dw.append(("TA", ta_g[G - 4]))
                s.emit("vector",
                       (lambda G: lambda e: e.tensor_copy(
                           Ph[1][G % 4], Pf[1][G % 4]))(G),
                       waits=dw, inc="VPH2")
                vph_g.append((vph, s.cur("VPH2")))
                pw = [("VPH", vph), ("VPH2", vph_g[-1][1])]
                if G >= 4:
                    pw.append(("TA", ta_g[G - 4]))
                s.emit("vector",
                       (lambda G: lambda e: e.tensor_sub(
                           Pl[0][G % 4], Pf[0][G % 4], Ph[0][G % 4]))(G),
                       waits=pw, inc="VP")
                s.emit("gpsimd",
                       (lambda G: lambda e: e.tensor_sub(
                           Pl[1][G % 4], Pf[1][G % 4], Ph[1][G % 4]))(G),
                       waits=pw, inc="VPB")
                vp_g.append((s.cur("VP"), s.cur("VPB")))
                if jt >= 2 and pend_evac:
                    pend_evac.pop(0)()
                pend_av.append((G, jt, qc))
            if qc == NQC - 1:
                while pend_av:
                    g0 = pend_av.pop(0)
                    emit_av(*g0)
                    if g0[1] == NJT - 1:
                        ta_toks[p] = s.cur("TA")
                        while pend_rel:
                            pend_rel.pop(0)()

            def make_release(qc=qc, p=p, bi=bi):
              def rel():
                ta_tok = s.cur("TA")
                idx = p * NQC + qc
                ounb = oun[qc % 2]
                ossl = ost[qc % 2]
                cell = {}
                rw = [("TA", ta_tok)]
                if idx >= 2:
                    rw.append(("VO", vo_toks[idx - 2]))
                if idx >= 1:
                    rw.append(("SPD", spd_toks[idx - 1]))
                s.emit("vector", (lambda ounb: lambda e: e.tensor_copy(
                    ounb[0:64, 0:512], banks[4][0:64, :]))(ounb), waits=rw)
                s.emit("vector", (lambda ounb: lambda e: e.tensor_copy(
                    ounb[0:64, 512:1024], banks[5][0:64, :]))(ounb))
                s.emit("vector", lambda e: e.tensor_copy(
                    dcp[64:65, 0:512], banks[4][64:65, :]))
                s.emit("vector", lambda e: e.tensor_copy(
                    dcp[64:65, 512:1024], banks[5][64:65, :]), inc="VOC")
                voc_toks.append(s.cur("VOC"))

                def mk_r2(quarter):
                    def r2():
                        if quarter == 0:
                            s.emit("vector", lambda e: e.drain())
                        c0 = quarter * 256
                        s.emit("vector", (lambda c0: lambda e: e.reciprocal(
                            rden[64:65, c0:c0 + 256],
                            dcp[64:65, c0:c0 + 256]))(c0),
                            inc=("VRD" if quarter == 3 else None))
                        if quarter == 3:
                            cell["vrd"] = s.cur("VRD")
                    return r2

                def r34():
                    s.emit("gpsimd", (lambda p, qc: lambda e: e.dma_start(
                        out=sc_den[p, qc, :], in_=rden[64:65, 0:1024]))(p, qc),
                        waits=[("VRD", cell["vrd"])], inc="SPD", amt=16)
                    cell["spd1"] = s.cur("SPD")
                    bw = [("SPD", cell["spd1"])]
                    if idx >= 1:
                        bw.append(("VO", vo_toks[idx - 1]))
                    s.emit("gpsimd", (lambda p, qc: lambda e: e.dma_start(
                        out=rbs[0:64, 0:1024],
                        in_=sc_den[p, qc, :].unsqueeze(0)
                        .broadcast_to([64, 1024])))(p, qc),
                        waits=bw, inc="SPD", amt=16)
                    spd_toks.append(s.cur("SPD"))
                    cell["spd_all"] = s.cur("SPD")

                def g1():
                    ew = [("SPD", cell["spd_all"])]
                    if idx >= 2:
                        ew.append(("SPO%d" % (qc % 2), spo_toks[idx - 2]))
                    s.emit("vector", lambda e: e.tensor_mul(
                        t1s[0:64, 0:1024], ounb[0:64, 0:1024],
                        rbs[0:64, 0:1024]), waits=ew)
                    s.emit("vector", lambda e: e.drain())

                def g2():
                    for hh in range(2):
                        hcol = 2 * p + hh
                        s.emit("vector",
                               (lambda hcol, ossl, hh: lambda e: e.tensor_scalar(
                                   ossl[0:64, hh * 1024:hh * 1024 + 512],
                                   t1s[0:64, hh * 512:hh * 512 + 512],
                                   bvsh[0:64, hcol:hcol + 1], None,
                                   ALU.add))(hcol, ossl, hh))
                    s.emit("vector", lambda e: e.drain())

                def g3():
                    for hh in range(2):
                        hcol = 2 * p + hh
                        s.emit("vector",
                               (lambda hcol, ossl, hh: lambda e:
                                e.scalar_tensor_tensor(
                                    ossl[0:64, hh * 1024 + 512:hh * 1024 + 1024],
                                    t1s[0:64, hh * 512:hh * 512 + 512],
                                    bvsh[0:64, hcol:hcol + 1],
                                    ossl[0:64, hh * 1024:hh * 1024 + 512],
                                    ALU.add, ALU.subtract))(hcol, ossl, hh),
                               inc=("VO" if hh == 1 else None))
                    vo_toks.append(s.cur("VO"))
                    vo = s.cur("VO")
                    for hh in range(2):
                        hcol = 2 * p + hh
                        s.emit("gpsimd",
                               (lambda hcol, qc, ossl, hh: lambda e: e.dma_start(
                                   out=sc_o[hcol, :, :, qc * 512:(qc + 1) * 512],
                                   in_=ossl[0:64, hh * 1024:hh * 1024 + 1024]
                                   .rearrange("p (l t) -> p l t", l=2)))(hcol, qc,
                                                                         ossl, hh),
                               waits=[("VO", vo)], inc="SPO%d" % (qc % 2), amt=16)
                    spo_toks.append(s.cur("SPO%d" % (qc % 2)))
                pend_evac.extend([mk_r2(0), mk_r2(1), mk_r2(2), mk_r2(3),
                                  r34, g1, g2, g3])
              return rel
            pend_rel.append(make_release())
            if qc == NQC - 1:
                while pend_rel:
                    pend_rel.pop(0)()

    flush_evac()

    wo_hi, wo_lo = bsl(0, 8192), bsl(8192, 8192)
    w8b = bsl(16384, 8192)
    obuf_hi = [bsl(24576, 1024), bsl(25600, 1024)]
    obuf_lo = [bsl(26624, 1024), bsl(27648, 1024)]
    xqb2 = [bsl(28672, 1024), bsl(29696, 1024)]

    w8s = bsl(49152, 4096).bitcast(I8)
    borep = fsl(8192, 1024)
    wsrep = fsl(9216, 1024)
    birep = fsl(10240, 1024)
    w3 = [("TA", s.cur("TA")), ("VO", s.cur("VO")), ("SPD", s.cur("SPD")),
          ("SPO0", s.cur("SPO0")), ("SPO1", s.cur("SPO1")),
          ("SCE", s.cur("SCE")), ("VP", s.cur("VP")),
          ("VPB", s.cur("VPB")), ("VPH", s.cur("VPH")),
          ("VPH2", s.cur("VPH2")), ("VOC", s.cur("VOC"))]
    s.emit("sync", lambda e: e.dma_start(out=borep, in_=BOREP),
           waits=w3, inc="LDB", amt=16)
    s.emit("sync", lambda e: e.dma_start(out=wsrep, in_=WSREP), inc="LDB", amt=16)
    s.emit("sync", lambda e: e.dma_start(out=birep, in_=BIREP), inc="LDB", amt=16)
    ldb = s.cur("LDB")
    s.emit("sync", lambda e: e.dma_start(out=wstage, in_=WOT), inc="LDW", amt=16)
    ldw = s.cur("LDW")
    s.emit("vector", lambda e: e.tensor_copy(wo_hi, wstage), waits=[("LDW", ldw)])
    s.emit("vector", lambda e: e.drain())
    s.emit("vector", lambda e: e.tensor_sub(wo_lo, wstage, wo_hi), inc="VW3")
    s.emit("sync", lambda e: e.dma_start(out=w8s, in_=W8T), inc="LDW8", amt=16)
    ldw8 = s.cur("LDW8")
    s.emit("vector", lambda e: e.tensor_copy(w8b, w8s),
           waits=[("LDW8", ldw8)], inc="VW3")
    vw3 = s.cur("VW3")

    xt = [fsl(0, 1024), fsl(1024, 1024)]
    tws = fsl(2048, 1024)
    g3b = fsl(3072, 1024)
    gb = fsl(4096, 1024)
    ssb = fsl(5120, 1024)
    outsb = [fsl(6144, 1024), fsl(7168, 1024)]

    tp3_toks = []
    ve_toks = []
    spout_toks = []
    spo_all = [("SPO0", s.cur("SPO0")), ("SPO1", s.cur("SPO1"))]
    spxq_all = [("SPXQ0", s.cur("SPXQ0")), ("SPXQ1", s.cur("SPXQ1"))]
    for tt in range(16):
        bi = tt % 2
        w = [("VW3", vw3)] + spo_all + spxq_all
        if tt >= 2:
            w.append(("TP3", tp3_toks[tt - 2]))
            w.append(("SPOUT%d" % (tt % 2), spout_toks[tt - 2]))
        for lh, dst in ((0, obuf_hi[bi]), (1, obuf_lo[bi])):
            for par in range(2):
                s.emit("sync",
                       (lambda lh, par, dst, tt: lambda e: e.dma_start(
                           out=dst[par * 64:(par + 1) * 64, :]
                           .rearrange("p (g c) -> p g c", g=8),
                           in_=sc_o[par::2, :, lh, tt * 128:(tt + 1) * 128]
                           .transpose([1, 0, 2])))(lh, par, dst, tt),
                       waits=(w if (lh == 0 and par == 0) else ()),
                       inc="LD3%d" % (tt % 2), amt=16)
        s.emit("sync",
               (lambda bi, tt: lambda e: e.dma_start(
                   out=xqb2[bi].rearrange("p (g c) -> p g c", g=8),
                   in_=sc_xq[:, :, tt * 128:(tt + 1) * 128]
                   .transpose([1, 0, 2])))(bi, tt), inc="LD3%d" % (tt % 2), amt=16)
        s.emit("sync",
               (lambda bi, tt: lambda e: e.dma_start(
                   out=xt[bi], in_=XOWN[tt * 128:(tt + 1) * 128, :]))(bi, tt),
               inc="LD3%d" % (tt % 2), amt=16)
        ld3 = s.cur("LD3%d" % (tt % 2))
        for fh in range(2):
            psW = banks[0 + fh + 2 * bi]
            psG = banks[4 + fh + 2 * bi]
            n_mm = 0
            for g in range(8):
                for (ob, wsrc) in ((obuf_hi[bi], wo_hi), (obuf_lo[bi], wo_hi),
                                   (obuf_hi[bi], wo_lo)):
                    st, sp = n_mm == 0, n_mm == 23
                    wl = []
                    if st:
                        wl = [("LD3%d" % (tt % 2), ld3)]
                        if tt >= 2:
                            wl.append(("VE", ve_toks[tt - 2]))
                        if tt < 2:
                            wl += [("SCE", s.cur("SCE")), ("VP", s.cur("VP")),
                                   ("VPB", s.cur("VPB")), ("VPH", s.cur("VPH")),
                                   ("VPH2", s.cur("VPH2")), ("VO", s.cur("VO"))]
                    s.emit("tensor",
                           (lambda psW, ob, wsrc, g, fh, st, sp: lambda e:
                            e.matmul(psW, ob[:, g * 128:(g + 1) * 128],
                                     wsrc[:, g * 1024 + fh * 512:
                                          g * 1024 + fh * 512 + 512],
                                     start=st, stop=sp))(psW, ob, wsrc, g, fh, st, sp),
                           waits=wl, inc=("TP3" if sp else None))
                    n_mm += 1
            for fc in range(8):
                s.emit("tensor",
                       (lambda psG, bi, fc, fh: lambda e: e.matmul(
                           psG, xqb2[bi][:, fc * 128:(fc + 1) * 128],
                           w8b[:, fc * 1024 + fh * 512:fc * 1024 + fh * 512 + 512],
                           start=(fc == 0), stop=(fc == 7)))(psG, bi, fc, fh),
                       inc=("TP3G" if fc == 7 else None))
            tp3 = s.cur("TP3")
            tp3g = s.cur("TP3G")
            sl0, sl1 = fh * 512, fh * 512 + 512

            s.emit("vector",
                   (lambda psG, sl0, sl1: lambda e: e.tensor_mul(
                       tws[:, sl0:sl1], psG, wsrep[:, sl0:sl1]))(psG, sl0, sl1),
                   waits=[("TP3G", tp3g), ("TP3", tp3)] +
                         ([("LDB", ldb)] if tt < 2 else []))
            s.emit("vector",
                   (lambda psW, sl0, sl1: lambda e: e.tensor_add(
                       ssb[:, sl0:sl1], psW, borep[:, sl0:sl1]))(psW, sl0, sl1))
            s.emit("vector", lambda e: e.drain())
            s.emit("vector",
                   (lambda sl0, sl1, tt: lambda e: e.scalar_tensor_tensor(
                       g3b[:, sl0:sl1], tws[:, sl0:sl1], xsr[:, tt:tt + 1],
                       birep[:, sl0:sl1], ALU.mult, ALU.add))(sl0, sl1, tt),
                   inc="VG")
            vg = s.cur("VG")
            s.emit("scalar",
                   (lambda sl0, sl1: lambda e: e.activation(
                       gb[:, sl0:sl1], g3b[:, sl0:sl1], AF.Silu))(sl0, sl1),
                   waits=[("VG", vg)], inc="SCG")
            scg = s.cur("SCG")
            s.emit("vector",
                   (lambda sl0, sl1: lambda e: e.tensor_mul(
                       tws[:, sl0:sl1], ssb[:, sl0:sl1], gb[:, sl0:sl1]))(sl0, sl1),
                   waits=[("SCG", scg)])
            s.emit("vector", lambda e: e.drain())
            ow = []
            if tt >= 2 and fh == 0:
                ow.append(("SPOUT%d" % (tt % 2), spout_toks[tt - 2]))
            s.emit("vector",
                   (lambda sl0, sl1, bi: lambda e: e.tensor_add(
                       outsb[bi][:, sl0:sl1], tws[:, sl0:sl1],
                       xt[bi][:, sl0:sl1]))(sl0, sl1, bi),
                   waits=ow, inc="VE")
        tp3_toks.append(s.cur("TP3"))
        ve_toks.append(s.cur("VE"))
        ve = s.cur("VE")
        s.emit("gpsimd",
               (lambda bi, tt: lambda e: e.dma_start(
                   out=OUT[tt * 128:(tt + 1) * 128, :], in_=outsb[bi]))(bi, tt),
               waits=[("VE", ve)], inc="SPOUT%d" % (tt % 2), amt=16)
        spout_toks.append(s.cur("SPOUT%d" % (tt % 2)))
    s.emit("gpsimd", lambda e: e.nop(),
           waits=[("SPOUT0", s.cur("SPOUT0")), ("SPOUT1", s.cur("SPOUT1"))])

    semnames = sorted(s.ct.keys())
    with ExitStack() as stack:
        sems = {n: stack.enter_context(nc.semaphore(n)) for n in semnames}
        block = stack.enter_context(nc.Block())

        def runner(lst):
            def run(e):
                last = {}
                for fn, waits, incn, amt in lst:
                    for sn, val in waits:
                        if val > last.get(sn, 0):
                            e.wait_ge(sems[sn], val)
                            last[sn] = val
                    ins = fn(e)
                    if incn is not None:
                        ins.then_inc(sems[incn], amt)
            return run

        block.sync(runner(s.ev["sync"]))
        block.gpsimd(runner(s.ev["gpsimd"]))
        block.tensor(runner(s.ev["tensor"]))
        block.scalar(runner(s.ev["scalar"]))
        block.vector(runner(s.ev["vector"]))
    return nc


def _pack_w(w):
    return np.ascontiguousarray(
        w.reshape(8, 128, 1024).transpose(1, 0, 2).reshape(128, 8192))


def kernel(L0_attn_x, L0_attn_Wq, L0_attn_bq, L0_attn_Wk, L0_attn_bk,
           L0_attn_Wv, L0_attn_bv, L0_attn_Wo, L0_attn_bo,
           L0_ffn_weight_q, L0_ffn_weight_scale, L0_ffn_bias):
    x = np.ascontiguousarray(np.asarray(L0_attn_x), dtype=np.float32)
    Wq = np.asarray(L0_attn_Wq, dtype=np.float32)
    Wk = np.asarray(L0_attn_Wk, dtype=np.float32)
    Wv = np.asarray(L0_attn_Wv, dtype=np.float32)
    Wo = np.asarray(L0_attn_Wo, dtype=np.float32)
    bq = np.asarray(L0_attn_bq, dtype=np.float32)
    bk = np.asarray(L0_attn_bk, dtype=np.float32)
    bv = np.asarray(L0_attn_bv, dtype=np.float32)
    bo = np.asarray(L0_attn_bo, dtype=np.float32)
    w8 = np.asarray(L0_ffn_weight_q)
    ws = np.asarray(L0_ffn_weight_scale, dtype=np.float32)
    bi_ = np.asarray(L0_ffn_bias, dtype=np.float32)

    WQT = _pack_w(np.ascontiguousarray(Wq.T))
    WKT = _pack_w(np.ascontiguousarray(Wk.T))
    WVT = _pack_w(np.ascontiguousarray(Wv.T))
    WOT = _pack_w(np.ascontiguousarray(Wo.T))
    W8T = _pack_w(np.ascontiguousarray(w8.T)).astype(np.int8)
    IDF = np.eye(128, dtype=np.float32)
    IDB = np.eye(128, dtype=ml_dtypes.bfloat16)
    ONESF = np.ones((1, 128), dtype=np.float32)
    INDEV = np.zeros((128, 1), dtype=ml_dtypes.bfloat16)
    INDEV[0:64] = 1
    INDOD = np.zeros((128, 1), dtype=ml_dtypes.bfloat16)
    INDOD[64:128] = 1
    ONESB = np.ones((128, 1), dtype=ml_dtypes.bfloat16)
    BQK = np.zeros((128, 16), dtype=np.float32)
    BQK[:, 0:8] = bq.reshape(8, 128).T
    BQK[:, 8:16] = bk.reshape(8, 128).T
    BVSH = np.zeros((128, 16), dtype=np.float32)
    for h in range(16):
        BVSH[0:64, h] = bv[h * 64:(h + 1) * 64]
    BOREP = np.ascontiguousarray(np.broadcast_to(bo, (128, 1024))).astype(np.float32)
    WSREP = np.ascontiguousarray(np.broadcast_to(ws, (128, 1024))).astype(np.float32)
    BIREP = np.ascontiguousarray(np.broadcast_to(bi_, (128, 1024))).astype(np.float32)

    nc = build_nc()
    in_maps = []
    for c in range(NC):
        b, h = c // 2, c % 2
        in_maps.append({
            "XOWN": np.ascontiguousarray(x[b, h * TOWN:(h + 1) * TOWN]),
            "XKV": np.ascontiguousarray(x[b]),
            "WQT": WQT, "WKT": WKT, "WVT": WVT, "WOT": WOT, "W8T": W8T,
            "IDF": IDF, "IDB": IDB, "ONESF": ONESF,
            "INDEV": INDEV, "INDOD": INDOD, "ONESB": ONESB,
            "BQK": BQK, "BVSH": BVSH, "BOREP": BOREP,
            "WSREP": WSREP, "BIREP": BIREP,
        })
    res = run_bass_kernel_spmd(nc, in_maps, core_ids=list(range(NC)))
    out = np.empty((B, N, F), dtype=np.float32)
    for c in range(NC):
        b, h = c // 2, c % 2
        out[b, h * TOWN:(h + 1) * TOWN] = res.results[c]["OUT"]
    return out


# version 14
# speedup vs baseline: 1.0166x; 1.0166x over previous
import numpy as np
import ml_dtypes
from contextlib import ExitStack
import concourse.bass as bass
import concourse.mybir as mybir
from concourse.bass_utils import run_bass_kernel_spmd

F32 = mybir.dt.float32
BF16 = mybir.dt.bfloat16
I8 = mybir.dt.int8
AF = mybir.ActivationFunctionType
ALU = mybir.AluOpType
AXX = mybir.AxisListType.X

B, N, F, H, HD = 4, 4096, 1024, 16, 64
TOWN = N // 2
NC = 8
CK_KV = 512
CK_OWN = 256
NKV = N // CK_KV
NOWN = TOWN // CK_OWN
NPAIR = 8
NQC = 4
NJT = 32

ENGINES = ("sync", "gpsimd", "tensor", "scalar", "vector")


class Sched:
    def __init__(self):
        self.ev = {e: [] for e in ENGINES}
        self.ct = {}

    def cur(self, sem):
        return self.ct.get(sem, 0)

    def emit(self, eng, fn, waits=(), inc=None, amt=1):
        tok = None
        if inc is not None:
            tok = self.ct.get(inc, 0) + amt
            self.ct[inc] = tok
        self.ev[eng].append((fn, tuple((sn, v) for sn, v in waits if v and v > 0),
                             inc, amt))
        return tok


def build_nc(dbg=False):
    nc = bass.Bass("TRN2", target_bir_lowering=False, debug=False)

    XOWN = nc.dram_tensor("XOWN", [TOWN, F], F32, kind="ExternalInput").ap()
    XKV = nc.dram_tensor("XKV", [N, F], F32, kind="ExternalInput").ap()
    WQT = nc.dram_tensor("WQT", [128, 8192], F32, kind="ExternalInput").ap()
    WKT = nc.dram_tensor("WKT", [128, 8192], F32, kind="ExternalInput").ap()
    WVT = nc.dram_tensor("WVT", [128, 8192], F32, kind="ExternalInput").ap()
    WOT = nc.dram_tensor("WOT", [128, 8192], F32, kind="ExternalInput").ap()
    W8T = nc.dram_tensor("W8T", [128, 8192], I8, kind="ExternalInput").ap()
    IDF = nc.dram_tensor("IDF", [128, 128], F32, kind="ExternalInput").ap()
    IDB = nc.dram_tensor("IDB", [128, 128], BF16, kind="ExternalInput").ap()
    ONESF = nc.dram_tensor("ONESF", [1, 128], F32, kind="ExternalInput").ap()
    INDEV = nc.dram_tensor("INDEV", [128, 1], BF16, kind="ExternalInput").ap()
    INDOD = nc.dram_tensor("INDOD", [128, 1], BF16, kind="ExternalInput").ap()
    ONESB = nc.dram_tensor("ONESB", [128, 1], BF16, kind="ExternalInput").ap()
    BQK = nc.dram_tensor("BQK", [128, 16], F32, kind="ExternalInput").ap()
    BVSH = nc.dram_tensor("BVSH", [128, 16], F32, kind="ExternalInput").ap()
    BOREP = nc.dram_tensor("BOREP", [128, 1024], F32, kind="ExternalInput").ap()
    WSREP = nc.dram_tensor("WSREP", [128, 1024], F32, kind="ExternalInput").ap()
    BIREP = nc.dram_tensor("BIREP", [128, 1024], F32, kind="ExternalInput").ap()
    OUT = nc.dram_tensor("OUT", [TOWN, F], F32, kind="ExternalOutput").ap()

    kd = dict(kind="ExternalOutput") if dbg else {}
    sc_k = nc.dram_tensor("sc_k", [8, 128, 2, N], BF16, **kd).ap()
    sc_q = nc.dram_tensor("sc_q", [8, 128, 2, TOWN], BF16, **kd).ap()
    sc_v = nc.dram_tensor("sc_v", [N // 128, 128, 2, 1024], BF16, **kd).ap()
    sc_o = nc.dram_tensor("sc_o", [16, 64, 2, TOWN], BF16, **kd).ap()
    sc_xq = nc.dram_tensor("sc_xq", [8, 128, TOWN], BF16, **kd).ap()
    sc_den = nc.dram_tensor("sc_den", [8, 4, 1024], F32, **kd).ap()

    idf = nc.alloc_sbuf_tensor("idf", [128, 128], F32).ap()
    idb = nc.alloc_sbuf_tensor("idb", [128, 128], BF16).ap()
    onesf = nc.alloc_sbuf_tensor("onesf", [1, 128], F32).ap()
    indev = nc.alloc_sbuf_tensor("indev", [128, 1], BF16).ap()
    indod = nc.alloc_sbuf_tensor("indod", [128, 1], BF16).ap()
    onesb = nc.alloc_sbuf_tensor("onesb", [128, 1], BF16).ap()
    bqk = nc.alloc_sbuf_tensor("bqk", [128, 16], F32).ap()
    bvsh = nc.alloc_sbuf_tensor("bvsh", [128, 16], F32).ap()
    mneg = nc.alloc_sbuf_tensor("mneg", [128, 16], F32).ap()
    xsr = nc.alloc_sbuf_tensor("xsr", [128, 16], F32).ap()
    stats = nc.alloc_sbuf_tensor("stats", [128, 64], F32).ap()
    nmax = nc.alloc_sbuf_tensor("nmax", [128, 448], F32).ap()
    wbf = nc.alloc_sbuf_tensor("wbf", [128, 69632], BF16).ap()
    fpool = nc.alloc_sbuf_tensor("fpool", [128, 15360], F32).ap()
    banks = [nc.alloc_psum_tensor("bank%d" % i, [128, 512], F32).ap()
             for i in range(8)]

    def bsl(off, n):
        return wbf[:, off:off + n]

    def fsl(off, n):
        return fpool[:, off:off + n]

    s = Sched()

    tr_toks = []
    kq_toks = []
    vb_toks = []
    nk_toks = []
    nk_toks_t = []
    kst_toks = []
    qst_toks = []
    vst_toks = []
    xqst_toks = []

    def rot(lst, d=2):
        return lst[-d] if len(lst) >= d else None

    for dst, src in ((idf, IDF), (idb, IDB), (onesf, ONESF), (indev, INDEV),
                     (indod, INDOD), (onesb, ONESB), (bqk, BQK), (bvsh, BVSH)):
        s.emit("sync", (lambda d_, sr: lambda e: e.dma_start(out=d_, in_=sr))(dst, src),
               inc="LDC", amt=16)

    wstage = fsl(0, 8192)
    for wi, (WSRC, hi_off, lo_off) in enumerate(
            ((WQT, 0, 8192), (WKT, 16384, 24576), (WVT, 32768, 40960))):
        waits = [("VW", s.cur("VW"))] if wi > 0 else []
        s.emit("sync", (lambda sr: lambda e: e.dma_start(out=wstage, in_=sr))(WSRC),
               waits=waits, inc="LDW", amt=16)
        ldtok = s.cur("LDW")
        hi, lo = bsl(hi_off, 8192), bsl(lo_off, 8192)
        s.emit("vector", (lambda h: lambda e: e.tensor_copy(h, wstage))(hi),
               waits=[("LDW", ldtok)])
        s.emit("vector", lambda e: e.drain())
        s.emit("vector", (lambda l, h: lambda e: e.tensor_sub(l, wstage, h))(lo, hi),
               inc="VW")
    vw_done = s.cur("VW")

    def emit_ln(xb, xnb, ntiles, ldw_pair, sq_out, v_sem, extra_sc_waits=()):
        nt = ntiles
        ldsem, ld_tok = ldw_pair
        for i in range(nt):
            s.emit("vector",
                   (lambda i: lambda e: e.tensor_reduce(
                       stats[:, 0 + i:1 + i], xb[:, i * 1024:(i + 1) * 1024],
                       axis=AXX, op=ALU.add))(i),
                   waits=[(ldsem, ld_tok)] if i == 0 else ())
        for i in range(nt):
            if i > 0:
                s.emit("scalar", lambda e: e.drain())
            s.emit("scalar",
                   (lambda i: lambda e: e.activation(
                       sq_out[:, 0:1024], xb[:, i * 1024:(i + 1) * 1024],
                       AF.Square, accum_out=stats[:, 8 + i:9 + i]))(i),
                   waits=(list(extra_sc_waits) + [(ldsem, ld_tok)]) if i == 0 else (),
                   inc=("SCLN" if i == nt - 1 else None))
        sct = s.cur("SCLN")
        s.emit("vector", lambda e: e.drain())
        s.emit("vector", (lambda nt: lambda e: e.tensor_scalar(
            stats[:, 16:16 + nt], stats[:, 0:nt], 1.0 / F, None, ALU.mult))(nt),
            waits=[("SCLN", sct)])
        s.emit("vector", (lambda nt: lambda e: e.tensor_scalar(
            stats[:, 32:32 + nt], stats[:, 8:8 + nt], 1.0 / F, None, ALU.mult))(nt))
        s.emit("vector", lambda e: e.drain())
        s.emit("vector", (lambda nt: lambda e: e.tensor_mul(
            stats[:, 24:24 + nt], stats[:, 16:16 + nt], stats[:, 16:16 + nt]))(nt))
        s.emit("vector", lambda e: e.drain())
        s.emit("vector", (lambda nt: lambda e: e.tensor_sub(
            stats[:, 32:32 + nt], stats[:, 32:32 + nt], stats[:, 24:24 + nt]))(nt))
        s.emit("vector", lambda e: e.drain())
        s.emit("vector", (lambda nt: lambda e: e.tensor_scalar(
            stats[:, 40:40 + nt], stats[:, 32:32 + nt], 1e-5, None, ALU.add))(nt))
        s.emit("vector", lambda e: e.drain())
        s.emit("vector", (lambda nt: lambda e: e.reciprocal(
            stats[:, 48:48 + nt], stats[:, 40:40 + nt]))(nt), inc="VRC")
        vrc = s.cur("VRC")
        s.emit("scalar", (lambda nt: lambda e: e.activation(
            stats[:, 56:56 + nt], stats[:, 48:48 + nt], AF.Sqrt))(nt),
            waits=[("VRC", vrc)], inc="SCLN")
        sct = s.cur("SCLN")
        for it in range(1):
            s.emit("vector", (lambda nt: lambda e: e.tensor_mul(
                stats[:, 48:48 + nt], stats[:, 56:56 + nt], stats[:, 56:56 + nt]))(nt),
                waits=[("SCLN", sct)] if it == 0 else ())
            s.emit("vector", lambda e: e.drain())
            s.emit("vector", (lambda nt: lambda e: e.tensor_mul(
                stats[:, 48:48 + nt], stats[:, 48:48 + nt], stats[:, 40:40 + nt]))(nt))
            s.emit("vector", lambda e: e.drain())
            s.emit("vector", (lambda nt: lambda e: e.tensor_scalar(
                stats[:, 48:48 + nt], stats[:, 48:48 + nt], -0.5, 1.5,
                ALU.mult, ALU.add))(nt))
            s.emit("vector", lambda e: e.drain())
            s.emit("vector", (lambda nt: lambda e: e.tensor_mul(
                stats[:, 56:56 + nt], stats[:, 56:56 + nt], stats[:, 48:48 + nt]))(nt))
            s.emit("vector", lambda e: e.drain())
        for i in range(nt):
            s.emit("vector",
                   (lambda i: lambda e: e.tensor_scalar(
                       xnb[:, i * 1024:(i + 1) * 1024],
                       xb[:, i * 1024:(i + 1) * 1024],
                       stats[:, 16 + i:17 + i], stats[:, 56 + i:57 + i],
                       ALU.subtract, ALU.mult))(i),
                   waits=[("TT", s.cur("TT")), ("SCLN", sct)] if i == 0 else (),
                   inc=(v_sem if i == nt - 1 else None))
        return s.cur(v_sem)

    def emit_transposes(xnb, ntiles, first_waits, hi_dst, lo_dst, w):
        for fc in range(8):
            bk = banks[len(tr_toks) % 2]
            for i in range(ntiles):
                wl = []
                if i == 0:
                    if fc == 0:
                        wl += list(first_waits)
                    t = rot(tr_toks)
                    if t:
                        wl.append(("VT", t))
                s.emit("tensor",
                       (lambda fc, i, bk: lambda e: e.transpose(
                           bk[:, i * 128:(i + 1) * 128],
                           xnb[:, i * 1024 + fc * 128: i * 1024 + fc * 128 + 128],
                           idf))(fc, i, bk),
                       waits=wl, inc=("TT" if i == ntiles - 1 else None))
            tt = s.cur("TT")
            scw = [("TT", tt)]
            if fc == 0:
                scw += [("TK", s.cur("TK")), ("TV", s.cur("TV"))]
            s.emit("scalar",
                   (lambda fc, bk, w: lambda e: e.activation(
                       hi_dst[:, fc * w:fc * w + w], bk[:, 0:w], AF.Copy))(bk=bk, fc=fc, w=w),
                   waits=scw, inc="SCT")
            sct = s.cur("SCT")
            s.emit("vector",
                   (lambda fc, bk, w: lambda e: e.tensor_sub(
                       lo_dst[:, fc * w:fc * w + w], bk[:, 0:w],
                       hi_dst[:, fc * w:fc * w + w]))(bk=bk, fc=fc, w=w),
                   waits=[("SCT", sct)], inc="VT")
            tr_toks.append(s.cur("VT"))

    xbuf = fsl(0, 2048)
    xnbuf = fsl(2048, 2048)
    sqbuf = fsl(4096, 1024)
    xbuf2 = fsl(5120, 2048)
    xnbuf2 = fsl(7168, 2048)
    t0buf = fsl(9216, 2048)
    ubuf = fsl(11264, 2048)
    bbuf = fsl(13312, 2048)
    xnT_hi = bsl(49152, 2048)
    xnT_lo = bsl(51200, 2048)
    xoT_hi = bsl(53248, 2048)
    xoT_lo = bsl(55296, 2048)
    kstage = [bsl(57344, 1024), bsl(58368, 1024)]
    vstage = [bsl(59392, 2048), bsl(61440, 2048)]
    ksq = [bsl(63488, 512), bsl(64000, 512)]
    xqbf = bsl(64512, 2048)
    qstage = [bsl(66560, 1024), bsl(67584, 1024)]
    xqst = [bsl(68608, 512), bsl(69120, 512)]
    nsq_i = [0]
    KHI, KLO, VHI, VLO = 16384, 24576, 32768, 40960

    vtasks = []

    def flush_vtask(n=1):
        for _ in range(n):
            if vtasks:
                vtasks.pop(0)()

    def emit_nsq(sqsrc, width, cols, pend_nk):
        def go():
            for par, ind in ((0, indev), (1, indod)):
                bnk = banks[6 + len(nk_toks) % 2]
                wl = []
                t = rot(nk_toks)
                if t:
                    wl.append(("VNK", t))
                s.emit("tensor",
                       (lambda ind, bnk, width: lambda e: e.matmul(
                           bnk[0:1, 0:width], ind, sqsrc[:, 0:width],
                           start=True, stop=True))(ind, bnk, width),
                       waits=wl + [("SCQ", pend_nk)], inc="TNK")
                tnk = s.cur("TNK")
                col = cols[par]
                s.emit("vector",
                       (lambda bnk, col, width: lambda e: e.tensor_reduce(
                           nmax[0:1, col:col + 1], bnk[0:1, 0:width], axis=AXX,
                           op=ALU.max))(bnk, col, width),
                       waits=[("TNK", tnk)], inc="VNK")
                nk_toks.append(s.cur("VNK"))
            nk_toks_t.append(s.cur("TNK"))
        return go

    NKV2 = 16
    pending_nk = None
    vqz_toks = []
    xqtr_toks = []
    for c in range(NKV2):
        waits = ([("VW", vw_done), ("LDC", s.cur("LDC"))] if c == 0
                 else [("VKV", s.cur("VKV")), ("SCLN", s.cur("SCLN")),
                       ("VLAD", s.cur("VLAD"))])
        s.emit("sync",
               (lambda c: lambda e: e.dma_start(
                   out=xbuf.rearrange("p (i f) -> p i f", i=2),
                   in_=XKV.rearrange("(c i p) f -> c i p f", i=2, p=128)[c]
                   .transpose([1, 0, 2])))(c),
               waits=waits, inc="LDXK", amt=16)
        ld_tok = s.cur("LDXK")
        vkv = emit_ln(xbuf, xnbuf, 2, ("LDXK", ld_tok), sqbuf, "VKV")
        emit_transposes(xnbuf, 2, [("VKV", vkv)], xnT_hi, xnT_lo, 256)
        vxt = s.cur("VT")

        for dc in range(8):
            bk = banks[2 + len(kq_toks) % 2]
            n_mm = 0
            for fc in range(8):
                for (woff, xsrc) in ((KHI, xnT_hi), (KLO, xnT_hi), (KHI, xnT_lo)):
                    st, sp = n_mm == 0, n_mm == 23
                    wl = []
                    if st:
                        wl = [("VT", vxt)]
                        t = rot(kq_toks)
                        if t:
                            wl.append(("VK", t))
                    s.emit("tensor",
                           (lambda bk, woff, fc, dc, xsrc, st, sp: lambda e:
                            e.matmul(bk[:, 0:256],
                                     wbf[:, woff + fc * 1024 + dc * 128:
                                         woff + fc * 1024 + dc * 128 + 128],
                                     xsrc[:, fc * 256:(fc + 1) * 256],
                                     start=st, stop=sp))(bk, woff, fc, dc, xsrc, st, sp),
                           waits=wl, inc=("TK" if sp else None))
                    n_mm += 1
            if pending_nk is not None:
                pending_nk()
                pending_nk = None
            tk = s.cur("TK")
            kst = kstage[dc % 2]
            wl = [("TK", tk)]
            t = rot(kst_toks)
            if t:
                wl.append(("SPK%d" % (dc % 2), t))
            s.emit("scalar",
                   (lambda kst, bk, dc: lambda e: e.activation(
                       kst[:, 0:256], bk[:, 0:256], AF.Identity,
                       bias=bqk[:, 8 + dc:9 + dc], scale=1.0))(kst, bk, dc),
                   waits=wl, inc="SCK")
            sck = s.cur("SCK")
            sqb_ = ksq[nsq_i[0] % 2]
            nsq_i[0] += 1
            s.emit("scalar", lambda e: e.drain())
            s.emit("scalar",
                   (lambda kst, sqb_: lambda e: e.activation(
                       sqb_[:, 0:256], kst[:, 0:256], AF.Square))(kst, sqb_),
                   waits=[("TNK", rot(nk_toks_t, 2) or 0)], inc="SCQ")
            scq = s.cur("SCQ")
            s.emit("vector",
                   (lambda kst, bk, dc: lambda e: e.scalar_tensor_tensor(
                       kst[:, 512:768], bk[:, 0:256], bqk[:, 8 + dc:9 + dc],
                       kst[:, 0:256], ALU.add, ALU.subtract))(kst, bk, dc),
                   waits=[("SCK", sck)], inc="VK")
            kq_toks.append(s.cur("VK"))
            vk = s.cur("VK")
            flush_vtask()
            pending_nk = emit_nsq(sqb_, 256,
                                  ((2 * dc) * 16 + c, (2 * dc + 1) * 16 + c), scq)
            s.emit("gpsimd",
                   (lambda kst, dc, c: lambda e: e.dma_start(
                       out=sc_k[dc, :, :, c * 256:(c + 1) * 256],
                       in_=kst.rearrange("p (l t) -> p l t", l=2)[:, :, 0:256]))(kst, dc, c),
                   waits=[("VK", vk)], inc="SPK%d" % (dc % 2), amt=16)
            kst_toks.append(s.cur("SPK%d" % (dc % 2)))
        if pending_nk is not None:
            pending_nk()
            pending_nk = None

        for i in range(2):
            gi = c * 2 + i
            vst = vstage[gi % 2]
            for half in range(2):
                bk = banks[4 + len(vb_toks) % 2]
                n_mm = 0
                for fc in range(8):
                    for (xsrc, woff) in ((xnT_hi, VHI), (xnT_lo, VHI), (xnT_hi, VLO)):
                        st, sp = n_mm == 0, n_mm == 23
                        wl = []
                        if st:
                            t = rot(vb_toks)
                            if t:
                                wl.append(("VV", t))
                        s.emit("tensor",
                               (lambda bk, xsrc, fc, i, woff, half, st, sp: lambda e:
                                e.matmul(bk, xsrc[:, fc * 256 + i * 128:
                                                  fc * 256 + i * 128 + 128],
                                         wbf[:, woff + fc * 1024 + half * 512:
                                             woff + fc * 1024 + half * 512 + 512],
                                         start=st, stop=sp))(bk, xsrc, fc, i, woff,
                                                             half, st, sp),
                               waits=wl, inc=("TV" if sp else None))
                        n_mm += 1
                tv = s.cur("TV")
                wl = [("TV", tv)]
                t = rot(vst_toks)
                if t and half == 0:
                    wl.append(("SPV%d" % (gi % 2), t))
                s.emit("scalar",
                       (lambda vst, bk, half: lambda e: e.activation(
                           vst[:, half * 512:half * 512 + 512], bk,
                           AF.Copy))(vst, bk, half),
                       waits=wl, inc="SCV")
                scv = s.cur("SCV")
                s.emit("vector",
                       (lambda vst, bk, half: lambda e: e.tensor_sub(
                           vst[:, 1024 + half * 512:1024 + half * 512 + 512],
                           bk, vst[:, half * 512:half * 512 + 512]))(vst, bk, half),
                       waits=[("SCV", scv)], inc="VV")
                vb_toks.append(s.cur("VV"))
                flush_vtask()
            vv = s.cur("VV")
            s.emit("gpsimd",
                   (lambda vst, gi: lambda e: e.dma_start(
                       out=sc_v[gi],
                       in_=vst.rearrange("p (l d) -> p l d", l=2)))(vst, gi),
                   waits=[("VV", vv)], inc="SPV%d" % (gi % 2), amt=16)
            vst_toks.append(s.cur("SPV%d" % (gi % 2)))

        if c % 2 == 0:
            continue
        oc = c // 2
        ow = [("VKV", s.cur("VKV")), ("LDC", s.cur("LDC"))]
        if oc > 0:
            ow += [("VLAD", s.cur("VLAD")), ("SCLN", s.cur("SCLN"))]
        s.emit("sync",
               (lambda oc: lambda e: e.dma_start(
                   out=xbuf2.rearrange("p (i f) -> p i f", i=2),
                   in_=XOWN.rearrange("(c i p) f -> c i p f", i=2, p=128)[oc]
                   .transpose([1, 0, 2])))(oc),
               waits=ow, inc="LDXO", amt=16)
        ld_tok = s.cur("LDXO")
        vown = emit_ln(xbuf2, xnbuf2, 2, ("LDXO", ld_tok), sqbuf, "VOWN")
        s.emit("vector", lambda e: e.drain())
        for i in range(2):
            s.emit("vector",
                   (lambda i: lambda e: e.tensor_reduce(
                       stats[:, 0 + i:1 + i], xnbuf2[:, i * 1024:(i + 1) * 1024],
                       axis=AXX, op=ALU.max, apply_absolute_value=True))(i))
        s.emit("vector", lambda e: e.drain())
        s.emit("vector",
               (lambda oc: lambda e: e.tensor_scalar(
                   xsr[:, oc * 2:oc * 2 + 2], stats[:, 0:2], 1e-6, 1.0 / 127.0,
                   ALU.max, ALU.mult))(oc))
        s.emit("vector", lambda e: e.drain())
        s.emit("vector",
               (lambda oc: lambda e: e.reciprocal(
                   stats[:, 12:14], xsr[:, oc * 2:oc * 2 + 2]))(oc))
        s.emit("vector", lambda e: e.drain())
        for i in range(2):
            s.emit("vector",
                   (lambda i: lambda e: e.tensor_scalar(
                       t0buf[:, i * 1024:(i + 1) * 1024],
                       xnbuf2[:, i * 1024:(i + 1) * 1024],
                       stats[:, 12 + i:13 + i], 128.5, ALU.mult, ALU.add))(i),
                   inc=("VLAD" if i == 1 else None))
        emit_transposes(xnbuf2, 2, [("VOWN", vown)], xoT_hi, xoT_lo, 256)
        vxt = s.cur("VT")

        def mk_piece(fn):
            vtasks.append(fn)

        mk_piece(lambda: (s.emit("vector", lambda e: e.drain()),
                          s.emit("vector", lambda e: e.tensor_copy(ubuf, t0buf)),
                          s.emit("vector", lambda e: e.drain())))
        for cst in (128.0, 64.0, 32.0, 16.0, 8.0, 4.0, 2.0, 1.0):
            mk_piece((lambda cst: lambda: (
                s.emit("vector",
                       (lambda cst: lambda e: e.tensor_scalar(
                           bbuf, ubuf, cst, None, ALU.is_ge))(cst)),
                s.emit("vector", lambda e: e.drain())))(cst))
            mk_piece((lambda cst: lambda: (
                s.emit("vector",
                       (lambda cst: lambda e: e.scalar_tensor_tensor(
                           ubuf, bbuf, -cst, ubuf, ALU.mult, ALU.add))(cst)),
                s.emit("vector", lambda e: e.drain())))(cst))

        def xq_piece(oc=oc):
            wl = [("TT", xqtr_toks[oc - 1])] if oc > 0 else []
            s.emit("vector", (lambda: lambda e: e.scalar_tensor_tensor(
                xqbf, t0buf, -128.0, ubuf, ALU.add, ALU.subtract))(),
                waits=wl, inc="VQZ")
            vqz_toks.append(s.cur("VQZ"))
        mk_piece(xq_piece)

        for dc in range(8):
            bk = banks[2 + len(kq_toks) % 2]
            n_mm = 0
            for fc in range(8):
                for (woff, xsrc) in ((0, xoT_hi), (8192, xoT_hi), (0, xoT_lo)):
                    st, sp = n_mm == 0, n_mm == 23
                    wl = []
                    if st:
                        wl = [("VT", vxt)]
                        t = rot(kq_toks)
                        if t:
                            wl.append(("VK", t))
                    s.emit("tensor",
                           (lambda bk, woff, fc, dc, xsrc, st, sp: lambda e:
                            e.matmul(bk[:, 0:256],
                                     wbf[:, woff + fc * 1024 + dc * 128:
                                         woff + fc * 1024 + dc * 128 + 128],
                                     xsrc[:, fc * 256:(fc + 1) * 256],
                                     start=st, stop=sp))(bk, woff, fc, dc, xsrc, st, sp),
                           waits=wl, inc=("TK" if sp else None))
                    n_mm += 1
            if pending_nk is not None:
                pending_nk()
                pending_nk = None
            tk = s.cur("TK")
            qst = qstage[dc % 2]
            wl = [("TK", tk)]
            t = rot(qst_toks)
            if t:
                wl.append(("SPQ%d" % (dc % 2), t))
            s.emit("scalar",
                   (lambda qst, bk, dc: lambda e: e.activation(
                       qst[:, 0:256], bk[:, 0:256], AF.Identity,
                       bias=bqk[:, dc:dc + 1], scale=1.0))(qst, bk, dc),
                   waits=wl, inc="SCK")
            sck = s.cur("SCK")
            sqb_ = ksq[nsq_i[0] % 2]
            nsq_i[0] += 1
            s.emit("scalar", lambda e: e.drain())
            s.emit("scalar",
                   (lambda qst, sqb_: lambda e: e.activation(
                       sqb_[:, 0:256], qst[:, 0:256], AF.Square))(qst, sqb_),
                   waits=[("TNK", rot(nk_toks_t, 2) or 0)], inc="SCQ")
            scq = s.cur("SCQ")
            s.emit("vector",
                   (lambda qst, bk, dc: lambda e: e.scalar_tensor_tensor(
                       qst[:, 512:768], bk[:, 0:256], bqk[:, dc:dc + 1],
                       qst[:, 0:256], ALU.add, ALU.subtract))(qst, bk, dc),
                   waits=[("SCK", sck)], inc="VK")
            kq_toks.append(s.cur("VK"))
            vk = s.cur("VK")
            flush_vtask()
            pending_nk = emit_nsq(sqb_, 256,
                                  (256 + (2 * dc) * 8 + oc,
                                   256 + (2 * dc + 1) * 8 + oc), scq)
            s.emit("gpsimd",
                   (lambda qst, dc, oc: lambda e: e.dma_start(
                       out=sc_q[dc, :, :, oc * 256:(oc + 1) * 256],
                       in_=qst.rearrange("p (l t) -> p l t", l=2)[:, :, 0:256]))(qst, dc, oc),
                   waits=[("VK", vk)], inc="SPQ%d" % (dc % 2), amt=16)
            qst_toks.append(s.cur("SPQ%d" % (dc % 2)))
        if pending_nk is not None:
            pending_nk()
            pending_nk = None

        if oc > 0:
            cc = oc - 1
            for fc in range(8):
                bk = banks[len(tr_toks) % 2].bitcast(BF16)
                for i in range(2):
                    wl = []
                    if i == 0:
                        if fc == 0:
                            wl.append(("VQZ", vqz_toks[cc]))
                        t = rot(tr_toks)
                        if t:
                            wl.append(("VT", t))
                    s.emit("tensor",
                           (lambda fc, i, bk, cc: lambda e: e.transpose(
                               bk[:, i * 128:(i + 1) * 128],
                               xqbf[:, i * 1024 + fc * 128: i * 1024 + fc * 128 + 128],
                               idb))(fc, i, bk, cc),
                           waits=wl, inc=("TT" if i == 1 else None))
                tt = s.cur("TT")
                xst = xqst[fc % 2]
                wl = [("TT", tt)]
                t = rot(xqst_toks)
                if t:
                    wl.append(("SPXQ%d" % (fc % 2), t))
                s.emit("vector",
                       (lambda bk, xst: lambda e: e.tensor_copy(
                           xst[:, 0:256], bk[:, 0:256]))(bk, xst),
                       waits=wl, inc="VT")
                tr_toks.append(s.cur("VT"))
                vt = s.cur("VT")
                s.emit("gpsimd",
                       (lambda fc, cc, xst: lambda e: e.dma_start(
                           out=sc_xq[fc, :, cc * 256:(cc + 1) * 256],
                           in_=xst[:, 0:256]))(fc, cc, xst),
                       waits=[("VT", vt)], inc="SPXQ%d" % (fc % 2), amt=16)
                xqst_toks.append(s.cur("SPXQ%d" % (fc % 2)))
            xqtr_toks.append(s.cur("TT"))

    while vtasks:
        flush_vtask()
    cc = 7
    for fc in range(8):
        bk = banks[len(tr_toks) % 2].bitcast(BF16)
        for i in range(2):
            wl = []
            if i == 0:
                if fc == 0:
                    wl.append(("VQZ", vqz_toks[cc]))
                t = rot(tr_toks)
                if t:
                    wl.append(("VT", t))
            s.emit("tensor",
                   (lambda fc, i, bk: lambda e: e.transpose(
                       bk[:, i * 128:(i + 1) * 128],
                       xqbf[:, i * 1024 + fc * 128: i * 1024 + fc * 128 + 128],
                       idb))(fc, i, bk),
                   waits=wl, inc=("TT" if i == 1 else None))
        tt = s.cur("TT")
        xst = xqst[fc % 2]
        wl = [("TT", tt)]
        t = rot(xqst_toks)
        if t:
            wl.append(("SPXQ%d" % (fc % 2), t))
        s.emit("vector",
               (lambda bk, xst: lambda e: e.tensor_copy(
                   xst[:, 0:256], bk[:, 0:256]))(bk, xst),
               waits=wl, inc="VT")
        tr_toks.append(s.cur("VT"))
        vt = s.cur("VT")
        s.emit("gpsimd",
               (lambda fc, xst: lambda e: e.dma_start(
                   out=sc_xq[fc, :, cc * 256:(cc + 1) * 256],
                   in_=xst[:, 0:256]))(fc, xst),
               waits=[("VT", vt)], inc="SPXQ%d" % (fc % 2), amt=16)
        xqst_toks.append(s.cur("SPXQ%d" % (fc % 2)))

    vnk_all = s.cur("VNK")
    s.emit("vector", lambda e: e.tensor_reduce(
        nmax[0:1, 384:400].rearrange("p (h o) -> p h o", o=1),
        nmax[0:1, 0:256].rearrange("p (h c) -> p h c", h=16), axis=AXX,
        op=ALU.max), waits=[("VNK", vnk_all)])
    s.emit("vector", lambda e: e.tensor_reduce(
        nmax[0:1, 400:416].rearrange("p (h o) -> p h o", o=1),
        nmax[0:1, 256:384].rearrange("p (h c) -> p h c", h=16), axis=AXX,
        op=ALU.max))
    s.emit("vector", lambda e: e.drain())
    s.emit("vector", lambda e: e.tensor_add(
        nmax[0:1, 416:432], nmax[0:1, 384:400], nmax[0:1, 400:416]))
    s.emit("vector", lambda e: e.drain())
    s.emit("vector", lambda e: e.tensor_scalar(
        nmax[0:1, 432:448], nmax[0:1, 416:432], -1.02 / 64.0, -0.05,
        ALU.mult, ALU.add), inc="VM")
    vm = s.cur("VM")
    s.emit("tensor", lambda e: e.matmul(
        banks[6][:, 0:16], onesf[0:1, :], nmax[0:1, 432:448],
        start=True, stop=True),
        waits=[("VM", vm), ("VNK", s.cur("VNK"))], inc="TM")
    tm = s.cur("TM")
    s.emit("vector", lambda e: e.tensor_copy(mneg, banks[6][:, 0:16]),
           waits=[("TM", tm)], inc="VM")
    vm_done = s.cur("VM")

    kb = [bsl(0, 8192), bsl(8192, 8192)]
    qb = [bsl(16384, 4096), bsl(20480, 4096)]
    vbh = [bsl(24576, 4160), bsl(28736, 4160)]
    vbl = [bsl(32896, 4160), bsl(37056, 4160)]
    Ph = [[bsl(41216 + hh * 2048 + r * 512, 512) for r in range(4)]
          for hh in range(2)]
    Pl = [[bsl(57344 + hh * 2048 + r * 512, 512) for r in range(4)]
          for hh in range(2)]
    ost = [bsl(45312, 2048), bsl(47360, 2048)]
    Pf = [[fsl(hh * 2048 + r * 512, 512) for r in range(4)]
          for hh in range(2)]
    t1s = fsl(4096, 1024)
    rbs = fsl(5120, 1024)
    rden = fsl(6144, 1024)
    oun = [fsl(7168, 1024), fsl(8192, 1024)]
    dcp = fsl(9216, 1024)

    for bi_ in range(2):
        s.emit("vector", (lambda bi_: lambda e: e.memset(
            vbh[bi_].rearrange("p (h j c) -> p h j c", h=2, j=32)[:, :, :, 64:65],
            1.0))(bi_),
            waits=([("TK", s.cur("TK")), ("TV", s.cur("TV")),
                    ("TT", s.cur("TT"))] if bi_ == 0 else ()))
        s.emit("vector", (lambda bi_: lambda e: e.memset(
            vbl[bi_].rearrange("p (h j c) -> p h j c", h=2, j=32)[:, :, :, 64:65],
            0.0))(bi_), inc=("VONE" if bi_ == 1 else None))
    vones = s.cur("VONE")
    s.emit("vector", lambda e: e.nop(), waits=[("VM", vm_done)], inc="VMS")
    vms = s.cur("VMS")

    ta_toks = {}
    sce_g = []
    vp_g = []
    vph_g = []
    ta_g = []
    vo_toks = []
    voc_toks = []
    spd_toks = []
    spo_toks = []
    pend_evac = []

    def flush_evac():
        while pend_evac:
            pend_evac.pop(0)()

    for p in range(NPAIR):
        bi = p % 2
        w = [("VMS", vms), ("VM", vm_done),
             ("SPK0", s.cur("SPK0")), ("SPK1", s.cur("SPK1")),
             ("SPQ0", s.cur("SPQ0")), ("SPQ1", s.cur("SPQ1")),
             ("SPV0", s.cur("SPV0")), ("SPV1", s.cur("SPV1")),
             ("VNK", s.cur("VNK")), ("VT", s.cur("VT"))]
        if p >= 2:
            w.append(("TA", ta_toks[p - 2]))
        ldksem = "LDK%d" % bi
        s.emit("sync", (lambda p, bi: lambda e: e.dma_start(
            out=kb[bi], in_=sc_k[p].rearrange("p l t -> p (l t)")))(p, bi),
            waits=w, inc=ldksem, amt=16)
        s.emit("sync", (lambda p, bi: lambda e: e.dma_start(
            out=qb[bi], in_=sc_q[p].rearrange("p l t -> p (l t)")))(p, bi),
            inc=ldksem, amt=16)
        for hh in range(2):
            for lh, dstl in ((0, vbh), (1, vbl)):
                s.emit("sync", (lambda p, bi, hh, lh, dstl: lambda e: e.dma_start(
                    out=dstl[bi].rearrange("p (h j c) -> p h j c", h=2, j=32)
                    [:, hh, :, 0:64],
                    in_=sc_v[:, :, lh, p * 128 + hh * 64: p * 128 + hh * 64 + 64]
                    .transpose([1, 0, 2])))(p, bi, hh, lh, dstl),
                    inc=ldksem, amt=16)
        ldk = s.cur(ldksem)

        khi, klo = kb[bi][:, 0:4096], kb[bi][:, 4096:8192]
        qhi, qlo = qb[bi][:, 0:2048], qb[bi][:, 2048:4096]
        pend_av = []
        pend_rel = []
        for qc in range(NQC):

            def emit_av(avG, avjt, avqc, p=p, bi=bi):
                aw = [("VPH", vph_g[avG][0]), ("VPH2", vph_g[avG][1])]
                aw3 = [("VP", vp_g[avG][0]), ("VPB", vp_g[avG][1])]
                if avjt == 0:
                    idx = p * NQC + avqc
                    if idx >= 1:
                        aw.append(("VOC", voc_toks[idx - 1]))
                    if idx == 0:
                        aw += [("VV", s.cur("VV")), ("VNK", s.cur("VNK")),
                               ("TM", tm)]
                    if idx in (0, NQC):
                        aw.append(("VONE", vones))
                vh = vbh[bi].rearrange("p (h j c) -> p h j c", h=2, j=32)
                vl = vbl[bi].rearrange("p (h j c) -> p h j c", h=2, j=32)
                last = (avjt == NJT - 1)
                for ip, vv_ in enumerate((vh, vl, vh)):
                    for hh in range(2):
                        ob = banks[4 + hh]
                        pq = (Ph if ip < 2 else Pl)[hh][avG % 4]
                        wl2 = ()
                        if ip == 0 and hh == 0:
                            wl2 = aw
                        elif ip == 2 and hh == 0:
                            wl2 = aw3
                        s.emit("tensor",
                               (lambda ob, vv_, pq, hh, avjt, st, sp: lambda e:
                                e.matmul(ob[0:65, :],
                                         vv_[:, hh, avjt, 0:65], pq,
                                         start=st, stop=sp,
                                         tile_position=(0, 0)))(
                                   ob, vv_, pq, hh, avjt,
                                   (avjt == 0 and ip == 0),
                                   (last and ip == 2)),
                               waits=wl2,
                               inc=("TA" if (ip == 2 and hh == 1) else None))
                ta_g.append(s.cur("TA"))
            for jt in range(NJT):
                G = len(sce_g)
                sA = banks[jt % 2]
                sB = banks[2 + jt % 2]
                wl = [(ldksem, ldk)]
                if G >= 2:
                    wl.append(("VPH", vph_g[G - 2][0]))
                for ip, (kk, qq) in enumerate(((khi, qhi), (klo, qhi),
                                               (khi, qlo))):
                    for hh, (sbank, base, tp) in enumerate(
                            ((sA, 0, (0, 0)), (sB, 64, (64, 0)))):
                        st, sp = ip == 0, ip == 2
                        s.emit("tensor",
                               (lambda sbank, kk, qq, base, tp, jt, qc, st, sp:
                                lambda e: e.matmul(
                                    sbank,
                                    kk[base:base + 64, jt * 128:(jt + 1) * 128],
                                    qq[base:base + 64, qc * 512:(qc + 1) * 512],
                                    start=st, stop=sp, tile_position=tp))(
                                   sbank, kk, qq, base, tp, jt, qc, st, sp),
                               waits=(wl if (hh == 0 and ip == 0) else ()),
                               inc=("TS" if (hh == 1 and sp) else None))
                ts_tok = s.cur("TS")
                if len(pend_av) >= 2:
                    g0 = pend_av.pop(0)
                    emit_av(*g0)
                    if g0[1] == NJT - 1:
                        ta_toks[p] = s.cur("TA")
                        while pend_rel:
                            pend_rel.pop(0)()
                scw = [("TS", ts_tok)]
                if G >= 4:
                    scw.append(("VP", vp_g[G - 4][0]))
                    scw.append(("VPB", vp_g[G - 4][1]))
                    scw.append(("TA", ta_g[G - 4]))
                s.emit("scalar",
                       (lambda sbank, G, p: lambda e: e.activation(
                           Pf[0][G % 4], sbank, AF.Exp,
                           bias=mneg[:, 2 * p:2 * p + 1],
                           scale=0.03125))(sA, G, p),
                       waits=scw)
                s.emit("scalar",
                       (lambda sbank, G, p: lambda e: e.activation(
                           Pf[1][G % 4], sbank, AF.Exp,
                           bias=mneg[:, 2 * p + 1:2 * p + 2],
                           scale=0.03125))(sB, G, p),
                       inc="SCE")
                sce_g.append(s.cur("SCE"))
                s.emit("scalar",
                       (lambda sbank, G, p: lambda e: e.activation(
                           Ph[0][G % 4], sbank, AF.Exp,
                           bias=mneg[:, 2 * p:2 * p + 1],
                           scale=0.03125))(sA, G, p))
                s.emit("scalar",
                       (lambda sbank, G, p: lambda e: e.activation(
                           Ph[1][G % 4], sbank, AF.Exp,
                           bias=mneg[:, 2 * p + 1:2 * p + 2],
                           scale=0.03125))(sB, G, p),
                       inc="VPH")
                vph_g.append((s.cur("VPH"), 0))
                pw = [("VPH", vph_g[-1][0])]
                if G >= 4:
                    pw.append(("TA", ta_g[G - 4]))
                s.emit("vector",
                       (lambda G: lambda e: e.tensor_sub(
                           Pl[0][G % 4], Pf[0][G % 4], Ph[0][G % 4]))(G),
                       waits=pw, inc="VP")
                s.emit("gpsimd",
                       (lambda G: lambda e: e.tensor_sub(
                           Pl[1][G % 4], Pf[1][G % 4], Ph[1][G % 4]))(G),
                       waits=pw, inc="VPB")
                vp_g.append((s.cur("VP"), s.cur("VPB")))
                if jt >= 2 and pend_evac:
                    pend_evac.pop(0)()
                pend_av.append((G, jt, qc))
            if qc == NQC - 1:
                while pend_av:
                    g0 = pend_av.pop(0)
                    emit_av(*g0)
                    if g0[1] == NJT - 1:
                        ta_toks[p] = s.cur("TA")
                        while pend_rel:
                            pend_rel.pop(0)()

            def make_release(qc=qc, p=p, bi=bi):
              def rel():
                ta_tok = s.cur("TA")
                idx = p * NQC + qc
                ounb = oun[qc % 2]
                ossl = ost[qc % 2]
                cell = {}
                rw = [("TA", ta_tok)]
                if idx >= 2:
                    rw.append(("VO", vo_toks[idx - 2]))
                if idx >= 1:
                    rw.append(("SPD", spd_toks[idx - 1]))
                s.emit("vector", (lambda ounb: lambda e: e.tensor_copy(
                    ounb[0:64, 0:512], banks[4][0:64, :]))(ounb), waits=rw)
                s.emit("vector", (lambda ounb: lambda e: e.tensor_copy(
                    ounb[0:64, 512:1024], banks[5][0:64, :]))(ounb))
                s.emit("vector", lambda e: e.tensor_copy(
                    dcp[64:65, 0:512], banks[4][64:65, :]))
                s.emit("vector", lambda e: e.tensor_copy(
                    dcp[64:65, 512:1024], banks[5][64:65, :]), inc="VOC")
                voc_toks.append(s.cur("VOC"))

                def mk_r2(quarter):
                    def r2():
                        if quarter == 0:
                            s.emit("vector", lambda e: e.drain())
                        c0 = quarter * 256
                        s.emit("vector", (lambda c0: lambda e: e.reciprocal(
                            rden[64:65, c0:c0 + 256],
                            dcp[64:65, c0:c0 + 256]))(c0),
                            inc=("VRD" if quarter == 3 else None))
                        if quarter == 3:
                            cell["vrd"] = s.cur("VRD")
                    return r2

                def r34():
                    s.emit("gpsimd", (lambda p, qc: lambda e: e.dma_start(
                        out=sc_den[p, qc, :], in_=rden[64:65, 0:1024]))(p, qc),
                        waits=[("VRD", cell["vrd"])], inc="SPD", amt=16)
                    cell["spd1"] = s.cur("SPD")
                    bw = [("SPD", cell["spd1"])]
                    if idx >= 1:
                        bw.append(("VO", vo_toks[idx - 1]))
                    s.emit("gpsimd", (lambda p, qc: lambda e: e.dma_start(
                        out=rbs[0:64, 0:1024],
                        in_=sc_den[p, qc, :].unsqueeze(0)
                        .broadcast_to([64, 1024])))(p, qc),
                        waits=bw, inc="SPD", amt=16)
                    spd_toks.append(s.cur("SPD"))
                    cell["spd_all"] = s.cur("SPD")

                def g1():
                    ew = [("SPD", cell["spd_all"])]
                    if idx >= 2:
                        ew.append(("SPO%d" % (qc % 2), spo_toks[idx - 2]))
                    s.emit("vector", lambda e: e.tensor_mul(
                        t1s[0:64, 0:1024], ounb[0:64, 0:1024],
                        rbs[0:64, 0:1024]), waits=ew)
                    s.emit("vector", lambda e: e.drain())

                def g2():
                    for hh in range(2):
                        hcol = 2 * p + hh
                        s.emit("vector",
                               (lambda hcol, ossl, hh: lambda e: e.tensor_scalar(
                                   ossl[0:64, hh * 1024:hh * 1024 + 512],
                                   t1s[0:64, hh * 512:hh * 512 + 512],
                                   bvsh[0:64, hcol:hcol + 1], None,
                                   ALU.add))(hcol, ossl, hh))
                    s.emit("vector", lambda e: e.drain())

                def g3():
                    for hh in range(2):
                        hcol = 2 * p + hh
                        s.emit("vector",
                               (lambda hcol, ossl, hh: lambda e:
                                e.scalar_tensor_tensor(
                                    ossl[0:64, hh * 1024 + 512:hh * 1024 + 1024],
                                    t1s[0:64, hh * 512:hh * 512 + 512],
                                    bvsh[0:64, hcol:hcol + 1],
                                    ossl[0:64, hh * 1024:hh * 1024 + 512],
                                    ALU.add, ALU.subtract))(hcol, ossl, hh),
                               inc=("VO" if hh == 1 else None))
                    vo_toks.append(s.cur("VO"))
                    vo = s.cur("VO")
                    for hh in range(2):
                        hcol = 2 * p + hh
                        s.emit("gpsimd",
                               (lambda hcol, qc, ossl, hh: lambda e: e.dma_start(
                                   out=sc_o[hcol, :, :, qc * 512:(qc + 1) * 512],
                                   in_=ossl[0:64, hh * 1024:hh * 1024 + 1024]
                                   .rearrange("p (l t) -> p l t", l=2)))(hcol, qc,
                                                                         ossl, hh),
                               waits=[("VO", vo)], inc="SPO%d" % (qc % 2), amt=16)
                    spo_toks.append(s.cur("SPO%d" % (qc % 2)))
                pend_evac.extend([mk_r2(0), mk_r2(1), mk_r2(2), mk_r2(3),
                                  r34, g1, g2, g3])
              return rel
            pend_rel.append(make_release())
            if qc == NQC - 1:
                while pend_rel:
                    pend_rel.pop(0)()

    flush_evac()

    wo_hi, wo_lo = bsl(0, 8192), bsl(8192, 8192)
    w8b = bsl(16384, 8192)
    obuf_hi = [bsl(24576, 1024), bsl(25600, 1024)]
    obuf_lo = [bsl(26624, 1024), bsl(27648, 1024)]
    xqb2 = [bsl(28672, 1024), bsl(29696, 1024)]

    w8s = bsl(49152, 4096).bitcast(I8)
    borep = fsl(8192, 1024)
    wsrep = fsl(9216, 1024)
    birep = fsl(10240, 1024)
    w3 = [("TA", s.cur("TA")), ("VO", s.cur("VO")), ("SPD", s.cur("SPD")),
          ("SPO0", s.cur("SPO0")), ("SPO1", s.cur("SPO1")),
          ("SCE", s.cur("SCE")), ("VP", s.cur("VP")),
          ("VPB", s.cur("VPB")), ("VPH", s.cur("VPH")),
          ("VPH2", s.cur("VPH2")), ("VOC", s.cur("VOC"))]
    s.emit("sync", lambda e: e.dma_start(out=borep, in_=BOREP),
           waits=w3, inc="LDB", amt=16)
    s.emit("sync", lambda e: e.dma_start(out=wsrep, in_=WSREP), inc="LDB", amt=16)
    s.emit("sync", lambda e: e.dma_start(out=birep, in_=BIREP), inc="LDB", amt=16)
    ldb = s.cur("LDB")
    s.emit("sync", lambda e: e.dma_start(out=wstage, in_=WOT), inc="LDW", amt=16)
    ldw = s.cur("LDW")
    s.emit("vector", lambda e: e.tensor_copy(wo_hi, wstage), waits=[("LDW", ldw)])
    s.emit("vector", lambda e: e.drain())
    s.emit("vector", lambda e: e.tensor_sub(wo_lo, wstage, wo_hi), inc="VW3")
    s.emit("sync", lambda e: e.dma_start(out=w8s, in_=W8T), inc="LDW8", amt=16)
    ldw8 = s.cur("LDW8")
    s.emit("vector", lambda e: e.tensor_copy(w8b, w8s),
           waits=[("LDW8", ldw8)], inc="VW3")
    vw3 = s.cur("VW3")

    xt = [fsl(0, 1024), fsl(1024, 1024)]
    tws = fsl(2048, 1024)
    g3b = fsl(3072, 1024)
    gb = fsl(4096, 1024)
    ssb = fsl(5120, 1024)
    outsb = [fsl(6144, 1024), fsl(7168, 1024)]

    tp3_toks = []
    ve_toks = []
    spout_toks = []
    spo_all = [("SPO0", s.cur("SPO0")), ("SPO1", s.cur("SPO1"))]
    spxq_all = [("SPXQ0", s.cur("SPXQ0")), ("SPXQ1", s.cur("SPXQ1"))]
    for tt in range(16):
        bi = tt % 2
        w = [("VW3", vw3)] + spo_all + spxq_all
        if tt >= 2:
            w.append(("TP3", tp3_toks[tt - 2]))
            w.append(("SPOUT%d" % (tt % 2), spout_toks[tt - 2]))
        for lh, dst in ((0, obuf_hi[bi]), (1, obuf_lo[bi])):
            for par in range(2):
                s.emit("sync",
                       (lambda lh, par, dst, tt: lambda e: e.dma_start(
                           out=dst[par * 64:(par + 1) * 64, :]
                           .rearrange("p (g c) -> p g c", g=8),
                           in_=sc_o[par::2, :, lh, tt * 128:(tt + 1) * 128]
                           .transpose([1, 0, 2])))(lh, par, dst, tt),
                       waits=(w if (lh == 0 and par == 0) else ()),
                       inc="LD3%d" % (tt % 2), amt=16)
        s.emit("sync",
               (lambda bi, tt: lambda e: e.dma_start(
                   out=xqb2[bi].rearrange("p (g c) -> p g c", g=8),
                   in_=sc_xq[:, :, tt * 128:(tt + 1) * 128]
                   .transpose([1, 0, 2])))(bi, tt), inc="LD3%d" % (tt % 2), amt=16)
        s.emit("sync",
               (lambda bi, tt: lambda e: e.dma_start(
                   out=xt[bi], in_=XOWN[tt * 128:(tt + 1) * 128, :]))(bi, tt),
               inc="LD3%d" % (tt % 2), amt=16)
        ld3 = s.cur("LD3%d" % (tt % 2))
        for fh in range(2):
            psW = banks[0 + fh + 2 * bi]
            psG = banks[4 + fh + 2 * bi]
            n_mm = 0
            for g in range(8):
                for (ob, wsrc) in ((obuf_hi[bi], wo_hi), (obuf_lo[bi], wo_hi),
                                   (obuf_hi[bi], wo_lo)):
                    st, sp = n_mm == 0, n_mm == 23
                    wl = []
                    if st:
                        wl = [("LD3%d" % (tt % 2), ld3)]
                        if tt >= 2:
                            wl.append(("VE", ve_toks[tt - 2]))
                        if tt < 2:
                            wl += [("SCE", s.cur("SCE")), ("VP", s.cur("VP")),
                                   ("VPB", s.cur("VPB")), ("VPH", s.cur("VPH")),
                                   ("VPH2", s.cur("VPH2")), ("VO", s.cur("VO"))]
                    s.emit("tensor",
                           (lambda psW, ob, wsrc, g, fh, st, sp: lambda e:
                            e.matmul(psW, ob[:, g * 128:(g + 1) * 128],
                                     wsrc[:, g * 1024 + fh * 512:
                                          g * 1024 + fh * 512 + 512],
                                     start=st, stop=sp))(psW, ob, wsrc, g, fh, st, sp),
                           waits=wl, inc=("TP3" if sp else None))
                    n_mm += 1
            for fc in range(8):
                s.emit("tensor",
                       (lambda psG, bi, fc, fh: lambda e: e.matmul(
                           psG, xqb2[bi][:, fc * 128:(fc + 1) * 128],
                           w8b[:, fc * 1024 + fh * 512:fc * 1024 + fh * 512 + 512],
                           start=(fc == 0), stop=(fc == 7)))(psG, bi, fc, fh),
                       inc=("TP3G" if fc == 7 else None))
            tp3 = s.cur("TP3")
            tp3g = s.cur("TP3G")
            sl0, sl1 = fh * 512, fh * 512 + 512

            s.emit("vector",
                   (lambda psG, sl0, sl1: lambda e: e.tensor_mul(
                       tws[:, sl0:sl1], psG, wsrep[:, sl0:sl1]))(psG, sl0, sl1),
                   waits=[("TP3G", tp3g), ("TP3", tp3)] +
                         ([("LDB", ldb)] if tt < 2 else []))
            s.emit("vector",
                   (lambda psW, sl0, sl1: lambda e: e.tensor_add(
                       ssb[:, sl0:sl1], psW, borep[:, sl0:sl1]))(psW, sl0, sl1))
            s.emit("vector", lambda e: e.drain())
            s.emit("vector",
                   (lambda sl0, sl1, tt: lambda e: e.scalar_tensor_tensor(
                       g3b[:, sl0:sl1], tws[:, sl0:sl1], xsr[:, tt:tt + 1],
                       birep[:, sl0:sl1], ALU.mult, ALU.add))(sl0, sl1, tt),
                   inc="VG")
            vg = s.cur("VG")
            s.emit("scalar",
                   (lambda sl0, sl1: lambda e: e.activation(
                       gb[:, sl0:sl1], g3b[:, sl0:sl1], AF.Silu))(sl0, sl1),
                   waits=[("VG", vg)], inc="SCG")
            scg = s.cur("SCG")
            s.emit("vector",
                   (lambda sl0, sl1: lambda e: e.tensor_mul(
                       tws[:, sl0:sl1], ssb[:, sl0:sl1], gb[:, sl0:sl1]))(sl0, sl1),
                   waits=[("SCG", scg)])
            s.emit("vector", lambda e: e.drain())
            ow = []
            if tt >= 2 and fh == 0:
                ow.append(("SPOUT%d" % (tt % 2), spout_toks[tt - 2]))
            s.emit("vector",
                   (lambda sl0, sl1, bi: lambda e: e.tensor_add(
                       outsb[bi][:, sl0:sl1], tws[:, sl0:sl1],
                       xt[bi][:, sl0:sl1]))(sl0, sl1, bi),
                   waits=ow, inc="VE")
        tp3_toks.append(s.cur("TP3"))
        ve_toks.append(s.cur("VE"))
        ve = s.cur("VE")
        s.emit("gpsimd",
               (lambda bi, tt: lambda e: e.dma_start(
                   out=OUT[tt * 128:(tt + 1) * 128, :], in_=outsb[bi]))(bi, tt),
               waits=[("VE", ve)], inc="SPOUT%d" % (tt % 2), amt=16)
        spout_toks.append(s.cur("SPOUT%d" % (tt % 2)))
    s.emit("gpsimd", lambda e: e.nop(),
           waits=[("SPOUT0", s.cur("SPOUT0")), ("SPOUT1", s.cur("SPOUT1"))])

    semnames = sorted(s.ct.keys())
    with ExitStack() as stack:
        sems = {n: stack.enter_context(nc.semaphore(n)) for n in semnames}
        block = stack.enter_context(nc.Block())

        def runner(lst):
            def run(e):
                last = {}
                for fn, waits, incn, amt in lst:
                    for sn, val in waits:
                        if val > last.get(sn, 0):
                            e.wait_ge(sems[sn], val)
                            last[sn] = val
                    ins = fn(e)
                    if incn is not None:
                        ins.then_inc(sems[incn], amt)
            return run

        block.sync(runner(s.ev["sync"]))
        block.gpsimd(runner(s.ev["gpsimd"]))
        block.tensor(runner(s.ev["tensor"]))
        block.scalar(runner(s.ev["scalar"]))
        block.vector(runner(s.ev["vector"]))
    return nc


def _pack_w(w):
    return np.ascontiguousarray(
        w.reshape(8, 128, 1024).transpose(1, 0, 2).reshape(128, 8192))


def kernel(L0_attn_x, L0_attn_Wq, L0_attn_bq, L0_attn_Wk, L0_attn_bk,
           L0_attn_Wv, L0_attn_bv, L0_attn_Wo, L0_attn_bo,
           L0_ffn_weight_q, L0_ffn_weight_scale, L0_ffn_bias):
    x = np.ascontiguousarray(np.asarray(L0_attn_x), dtype=np.float32)
    Wq = np.asarray(L0_attn_Wq, dtype=np.float32)
    Wk = np.asarray(L0_attn_Wk, dtype=np.float32)
    Wv = np.asarray(L0_attn_Wv, dtype=np.float32)
    Wo = np.asarray(L0_attn_Wo, dtype=np.float32)
    bq = np.asarray(L0_attn_bq, dtype=np.float32)
    bk = np.asarray(L0_attn_bk, dtype=np.float32)
    bv = np.asarray(L0_attn_bv, dtype=np.float32)
    bo = np.asarray(L0_attn_bo, dtype=np.float32)
    w8 = np.asarray(L0_ffn_weight_q)
    ws = np.asarray(L0_ffn_weight_scale, dtype=np.float32)
    bi_ = np.asarray(L0_ffn_bias, dtype=np.float32)

    WQT = _pack_w(np.ascontiguousarray(Wq.T))
    WKT = _pack_w(np.ascontiguousarray(Wk.T))
    WVT = _pack_w(np.ascontiguousarray(Wv.T))
    WOT = _pack_w(np.ascontiguousarray(Wo.T))
    W8T = _pack_w(np.ascontiguousarray(w8.T)).astype(np.int8)
    IDF = np.eye(128, dtype=np.float32)
    IDB = np.eye(128, dtype=ml_dtypes.bfloat16)
    ONESF = np.ones((1, 128), dtype=np.float32)
    INDEV = np.zeros((128, 1), dtype=ml_dtypes.bfloat16)
    INDEV[0:64] = 1
    INDOD = np.zeros((128, 1), dtype=ml_dtypes.bfloat16)
    INDOD[64:128] = 1
    ONESB = np.ones((128, 1), dtype=ml_dtypes.bfloat16)
    BQK = np.zeros((128, 16), dtype=np.float32)
    BQK[:, 0:8] = bq.reshape(8, 128).T
    BQK[:, 8:16] = bk.reshape(8, 128).T
    BVSH = np.zeros((128, 16), dtype=np.float32)
    for h in range(16):
        BVSH[0:64, h] = bv[h * 64:(h + 1) * 64]
    BOREP = np.ascontiguousarray(np.broadcast_to(bo, (128, 1024))).astype(np.float32)
    WSREP = np.ascontiguousarray(np.broadcast_to(ws, (128, 1024))).astype(np.float32)
    BIREP = np.ascontiguousarray(np.broadcast_to(bi_, (128, 1024))).astype(np.float32)

    nc = build_nc()
    in_maps = []
    for c in range(NC):
        b, h = c // 2, c % 2
        in_maps.append({
            "XOWN": np.ascontiguousarray(x[b, h * TOWN:(h + 1) * TOWN]),
            "XKV": np.ascontiguousarray(x[b]),
            "WQT": WQT, "WKT": WKT, "WVT": WVT, "WOT": WOT, "W8T": W8T,
            "IDF": IDF, "IDB": IDB, "ONESF": ONESF,
            "INDEV": INDEV, "INDOD": INDOD, "ONESB": ONESB,
            "BQK": BQK, "BVSH": BVSH, "BOREP": BOREP,
            "WSREP": WSREP, "BIREP": BIREP,
        })
    res = run_bass_kernel_spmd(nc, in_maps, core_ids=list(range(NC)))
    out = np.empty((B, N, F), dtype=np.float32)
    for c in range(NC):
        b, h = c // 2, c % 2
        out[b, h * TOWN:(h + 1) * TOWN] = res.results[c]["OUT"]
    return out
